# version 02
# baseline (speedup 1.0000x reference)
import math
import numpy as np
from contextlib import ExitStack

import concourse.bass as bass
import concourse.mybir as mybir
from concourse.alu_op_type import AluOpType as ALU
from concourse.bass_utils import run_bass_kernel_spmd

F32 = mybir.dt.float32
BF16 = mybir.dt.bfloat16
AF = mybir.ActivationFunctionType
AX = mybir.AxisListType

NCORE = 8
BATCH = 2
SEQ = 8192
D = 1024
TC = SEQ * BATCH // NCORE
NCH = D // 128


def f32c(x):
    return np.ascontiguousarray(x, dtype=np.float32)


def chunk_vec(v):
    v = np.asarray(v, dtype=np.float32)
    m = v.shape[0]
    nch = (m + 127) // 128
    pad = nch * 128 - m
    if pad:
        v = np.concatenate([v, np.zeros(pad, np.float32)])
    return f32c(v.reshape(nch, 128).T)


class Prog:

    def __init__(self, name="prog"):
        self.nc = bass.Bass("TRN2", target_bir_lowering=False)
        self.es = ExitStack()
        self.sems = {}
        self.counts = {}
        self.in_arrays = [dict() for _ in range(NCORE)]
        self.out_names = []
        self._uid = 0

    def uid(self):
        self._uid += 1
        return self._uid

    def sem(self, name):
        if name not in self.sems:
            self.sems[name] = self.es.enter_context(self.nc.semaphore(name))
            self.counts[name] = 0
        return self.sems[name]

    def inc(self, instr, name, n=1):
        instr.then_inc(self.sem(name), n)
        self.counts[name] += n
        return self.counts[name]

    def val(self, name):
        self.sem(name)
        return self.counts[name]

    def wait(self, eng, name, value=None):
        eng.wait_ge(self.sem(name), self.counts[name] if value is None else value)

    def join_engines(self, engs=None, sems=None):
        engs = engs or [self.nc.tensor, self.nc.vector, self.nc.scalar,
                        self.nc.gpsimd, self.nc.sync]
        sems = sems or list(self.sems.keys())
        for e in engs:
            for s in sems:
                self.wait(e, s)

    def sbuf(self, name, shape, dt=F32):
        return self.es.enter_context(self.nc.sbuf_tensor(name, shape, dt))

    def psum(self, name, shape, dt=F32):
        return self.es.enter_context(self.nc.psum_tensor(name, shape, dt))

    def din(self, name, arrays, dt=F32):
        if isinstance(arrays, np.ndarray):
            arrays = [arrays] * NCORE
        a0 = arrays[0]
        ap = self.nc.dram_tensor(name, list(a0.shape), dt, kind="ExternalInput").ap()
        for c in range(NCORE):
            assert arrays[c].shape == a0.shape and arrays[c].dtype == a0.dtype, name
            self.in_arrays[c][name] = arrays[c]
        return ap

    def dout(self, name, shape, dt=F32):
        self.out_names.append(name)
        return self.nc.dram_tensor(name, list(shape), dt, kind="ExternalOutput").ap()

    def run(self, **kw):
        return run_bass_kernel_spmd(self.nc, self.in_arrays, core_ids=list(range(NCORE)), **kw)

    def close(self):
        self.es.close()


class PsumPool:

    def __init__(self, p, name, banks=6, width=512, dt=F32):
        self.p = p
        self.width = width
        self.tile = p.psum(name, [128, banks * width], dt)
        self.banks = banks
        self.tokens = [None] * banks
        self.i = 0

    def acquire(self):
        b = self.i % self.banks
        self.i += 1
        tok = self.tokens[b]
        if tok is not None:
            self.p.nc.tensor.wait_ge(self.p.sem(tok[0]), tok[1])
            self.tokens[b] = None
        return self.tile[:, b * self.width:(b + 1) * self.width], b

    def release(self, b, sem_name, value):
        self.tokens[b] = (sem_name, value)


class WStream:

    def __init__(self, p, name, elems):
        self.p = p
        self.elems = elems
        self.bufs = [p.sbuf(f"{name}{i}", [128, elems], BF16) for i in range(2)]
        self.reltok = [None, None]
        self.i = 0


def gemm(p, tag, *, lhsT_dram, K, M, T, rhs, epilogue, pool, ws,
         t_tile=512, m_group=None, rhs_ready=None, dma_eng=None):
    nc = p.nc
    nk = (K + 127) // 128
    nm = (M + 127) // 128
    ntt = (T + t_tile - 1) // t_tile
    if m_group is None:
        m_group = max(1, min(nm, 4096 // (nk * 128)))
    dma = dma_eng or nc.gpsimd
    slabm = m_group * 128
    assert nk * slabm <= ws.elems, (tag, nk, slabm, ws.elems)
    if rhs_ready is not None:
        nc.tensor.wait_ge(p.sem(rhs_ready[0]), rhs_ready[1])
    ngrp = (nm + m_group - 1) // m_group
    for g in range(ngrp):
        par = ws.i % 2
        ws.i += 1
        wbuf = ws.bufs[par]
        m0 = g * m_group
        mg = min(m_group, nm - m0)
        if ws.reltok[par] is not None:
            t0_, v0_ = ws.reltok[par]
            dma.wait_ge(p.sem(t0_), v0_)
        mcols = min(slabm, M - m0 * 128)
        if K % 128 == 0 and nk > 1:
            src = lhsT_dram[:, m0 * 128: m0 * 128 + mcols].rearrange(
                "(kc kp) m -> kp kc m", kp=128)
            dst = wbuf[:, 0: nk * slabm].rearrange("p (kc m) -> p kc m", m=slabm)
            dst = dst[:, :, 0:mcols]
            p.inc(dma.dma_start(out=dst, in_=src), "wld", 16)
        else:
            for kc in range(nk):
                krows = min(128, K - kc * 128)
                p.inc(dma.dma_start(
                    out=wbuf[0:krows, kc * slabm: kc * slabm + mcols],
                    in_=lhsT_dram[kc * 128: kc * 128 + krows,
                                  m0 * 128: m0 * 128 + mcols]), "wld", 16)
        nc.tensor.wait_ge(p.sem("wld"), p.val("wld"))
        for ml in range(mg):
            mo = m0 + ml
            mrows = min(128, M - mo * 128)
            for tt in range(ntt):
                t0 = tt * t_tile
                t1 = min(T, t0 + t_tile)
                psb, bank = pool.acquire()
                ps = psb[0:mrows, 0:t1 - t0]
                for kc in range(nk):
                    krows = min(128, K - kc * 128)
                    mm = nc.tensor.matmul(
                        ps,
                        wbuf[0:krows, kc * slabm + ml * 128: kc * slabm + ml * 128 + mrows],
                        rhs(kc, t0, t1),
                        start=(kc == 0), stop=(kc == nk - 1))
                p.inc(mm, "mm", 1)
                epilogue(mo, t0, t1, ps, pool, bank)
        ws.reltok[par] = ("mm", p.val("mm"))


def make_ln_scratch(p, t_tile=512):
    u = p.uid()
    return {
        "sq": [p.sbuf(f"lnsq{u}_{i}", [128, t_tile], F32) for i in range(2)],
        "app": p.sbuf(f"lnapp{u}", [128, t_tile], F32),
        "mean": p.sbuf(f"lnmean{u}", [1, t_tile], F32),
        "var": p.sbuf(f"lnvar{u}", [1, t_tile], F32),
        "m2": p.sbuf(f"lnm2{u}", [1, t_tile], F32),
        "ps_sum": p.psum(f"lnpsA{u}", [1, t_tile], F32),
        "ps_ssq": p.psum(f"lnpsB{u}", [1, t_tile], F32),
        "t_tile": t_tile,
        "sq_tok": [None, None],
    }


def layernorm(p, tag, *, X, T, scr, nch=NCH, Dm=None, out_bf=None, out_f32=None,
              gain=None, gbias=None, eps_ap=None, pool=None, ones_f32=None):
    nc = p.nc
    Dm = Dm or nch * 128
    t_tile = scr["t_tile"]
    sq, app = scr["sq"], scr["app"]
    mean_r, var_r, m2_r = scr["mean"], scr["var"], scr["m2"]
    ntt = (T + t_tile - 1) // t_tile
    for tt in range(ntt):
        t0 = tt * t_tile
        t1 = min(T, t0 + t_tile)
        w = t1 - t0
        ps_sum = scr["ps_sum"][0:1, 0:w]
        ps_ssq = scr["ps_ssq"][0:1, 0:w]
        nc.tensor.wait_ge(p.sem("ln_v"), p.val("ln_v"))
        for j in range(nch):
            par = j % 2
            if scr["sq_tok"][par] is not None:
                nc.scalar.wait_ge(p.sem("mm"), scr["sq_tok"][par])
            s = nc.scalar.activation(sq[par][:, 0:w], X(j, t0, t1), AF.Square)
            sqv = p.inc(s, "ln_s", 1)
            nc.tensor.matmul(ps_sum, ones_f32[:, 0:1], X(j, t0, t1),
                             start=(j == 0), stop=(j == nch - 1))
            nc.tensor.wait_ge(p.sem("ln_s"), sqv)
            mm = nc.tensor.matmul(ps_ssq, ones_f32[:, 0:1], sq[par][:, 0:w],
                                  start=(j == 0), stop=(j == nch - 1))
            scr["sq_tok"][par] = p.inc(mm, "mm", 1)
        mmv = p.val("mm")
        nc.vector.wait_ge(p.sem("mm"), mmv)
        v = nc.vector.tensor_scalar_mul(mean_r[0:1, 0:w], ps_sum, 1.0 / Dm)
        p.inc(v, "ln_v", 1)
        v = nc.vector.tensor_scalar_mul(var_r[0:1, 0:w], ps_ssq, 1.0 / Dm)
        p.inc(v, "ln_v", 1)
        v = nc.vector.tensor_tensor(m2_r[0:1, 0:w], mean_r[0:1, 0:w], mean_r[0:1, 0:w], ALU.mult)
        p.inc(v, "ln_v", 1)
        v = nc.vector.tensor_sub(var_r[0:1, 0:w], var_r[0:1, 0:w], m2_r[0:1, 0:w])
        vv = p.inc(v, "ln_v", 1)
        nc.scalar.wait_ge(p.sem("ln_v"), vv)
        s = nc.scalar.activation(var_r[0:1, 0:w], var_r[0:1, 0:w], AF.Ln, bias=eps_ap)
        p.inc(s, "ln_s", 1)
        s = nc.scalar.activation(var_r[0:1, 0:w], var_r[0:1, 0:w], AF.Exp, scale=-0.5)
        sv = p.inc(s, "ln_s", 1)
        psm, bank_m = pool.acquire()
        psr, bank_r = pool.acquire()
        nc.tensor.wait_ge(p.sem("ln_s"), sv)
        nc.tensor.matmul(psm[:, 0:w], ones_f32[0:1, :], mean_r[0:1, 0:w], start=True, stop=True)
        mm = nc.tensor.matmul(psr[:, 0:w], ones_f32[0:1, :], var_r[0:1, 0:w], start=True, stop=True)
        mmv = p.inc(mm, "mm", 1)
        nc.vector.wait_ge(p.sem("mm"), mmv)
        for j in range(nch):
            v = nc.vector.tensor_sub(app[:, 0:w], X(j, t0, t1), psm[:, 0:w])
            p.inc(v, "ln_v", 1)
            dsts = ([] if out_f32 is None else [out_f32(j, t0, t1)]) + \
                   ([] if out_bf is None else [out_bf(j, t0, t1)])
            first = dsts[0]
            v = nc.vector.tensor_tensor(first, app[:, 0:w], psr[:, 0:w], ALU.mult)
            p.inc(v, "ln_v", 1)
            if gain is not None:
                v = nc.vector.tensor_scalar(first, first, gain[:, j:j + 1],
                                            gbias[:, j:j + 1], ALU.mult, ALU.add)
                p.inc(v, "ln_v", 1)
            for d2 in dsts[1:]:
                v = nc.vector.tensor_copy(d2, first)
                p.inc(v, "ln_v", 1)
        vval = p.val("ln_v")
        pool.release(bank_m, "ln_v", vval)
        pool.release(bank_r, "ln_v", vval)


def emit_l1ffn(p, tag, *, x_dram, y_dram, W, consts, T=TC, t_blk=512):
    nc = p.nc
    u = p.uid()
    ones_f32 = consts["ones_f32"]
    pool = PsumPool(p, f"pp{u}", banks=6)
    DH = 4096
    nH = DH // 128
    xn = p.sbuf(f"xn{u}", [128, NCH * t_blk], BF16)
    xf = p.sbuf(f"xf{u}", [128, NCH * t_blk], F32)
    ha = p.sbuf(f"ha{u}", [128, t_blk], F32)
    hb = p.sbuf(f"hb{u}", [128, t_blk], F32)
    H = p.sbuf(f"H{u}", [128, nH * t_blk], BF16)
    H2 = p.sbuf(f"H2{u}", [128, nH * t_blk], BF16)
    yout = p.sbuf(f"y{u}", [128, NCH * t_blk], F32)
    scr = make_ln_scratch(p, t_blk if t_blk <= 512 else 512)
    ws = WStream(p, f"ws{u}", 4096)
    nblk = (T + t_blk - 1) // t_blk
    for blk in range(nblk):
        b0 = blk * t_blk
        b1 = min(T, b0 + t_blk)
        wb = b1 - b0
        p.wait(nc.sync, "ffO")
        p.wait(nc.sync, "ln_v")
        p.wait(nc.sync, "ln_s")
        p.wait(nc.sync, "mm")
        for j in range(NCH):
            p.inc(nc.sync.dma_start(out=xf[:, j * t_blk: j * t_blk + wb],
                                    in_=x_dram[j * 128:(j + 1) * 128, b0:b1]), "xld", 16)
        p.wait(nc.scalar, "xld")
        p.wait(nc.tensor, "xld")
        p.wait(nc.vector, "xld")
        p.wait(nc.vector, "yst")
        p.wait(nc.vector, "mm")
        layernorm(p, f"{tag}ln{blk}", X=lambda j, a, b: xf[:, j * t_blk + a: j * t_blk + b],
                  T=wb, scr=scr, out_bf=lambda j, a, b: xn[:, j * t_blk + a: j * t_blk + b],
                  pool=pool, ones_f32=ones_f32, eps_ap=consts["eps"][0:1, 0:1])
        xn_rhs = lambda kc, a, b: xn[0:128, kc * t_blk + a: kc * t_blk + b]
        H_rhs = lambda kc, a, b: H[0:128, kc * t_blk + a: kc * t_blk + b]
        H2_rhs = lambda kc, a, b: H2[0:128, kc * t_blk + a: kc * t_blk + b]

        def epA(mo, t0, t1, ps, pool_, bank):
            p.wait(nc.scalar, "mm")
            p.wait(nc.scalar, "ffB")
            s = nc.scalar.activation(ha[:, t0:t1], ps, AF.Silu, scale=W["w1s"][:, mo:mo + 1])
            pool_.release(bank, "ffA", p.inc(s, "ffA", 1))

        def epB(mo, t0, t1, ps, pool_, bank):
            p.wait(nc.scalar, "mm")
            p.wait(nc.scalar, "ffB")
            s = nc.scalar.activation(hb[:, t0:t1], ps, AF.Copy, scale=W["w2s"][:, mo:mo + 1])
            sv = p.inc(s, "ffB", 1)
            nc.vector.wait_ge(p.sem("ffB"), sv)
            nc.vector.wait_ge(p.sem("ffA"), p.val("ffA"))
            v = nc.vector.tensor_tensor(H[:, mo * t_blk + t0: mo * t_blk + t1],
                                        ha[:, t0:t1], hb[:, t0:t1], ALU.mult)
            pool_.release(bank, "ffB", p.inc(v, "ffB", 1))

        lnv = ("ln_v", p.val("ln_v"))
        for mo in range(nH):
            gemm(p, f"{tag}w1", lhsT_dram=W["w1T"][:, mo * 128:(mo + 1) * 128],
                 K=D, M=128, T=wb, rhs=xn_rhs, epilogue=lambda m_, a, b, ps, pl, bk, mo=mo:
                     epA(mo, a, b, ps, pl, bk),
                 pool=pool, ws=ws, t_tile=t_blk, rhs_ready=lnv)
            gemm(p, f"{tag}w2", lhsT_dram=W["w2T"][:, mo * 128:(mo + 1) * 128],
                 K=D, M=128, T=wb, rhs=xn_rhs, epilogue=lambda m_, a, b, ps, pl, bk, mo=mo:
                     epB(mo, a, b, ps, pl, bk),
                 pool=pool, ws=ws, t_tile=t_blk, rhs_ready=lnv)

        def epG(scale_tile, dst, semname):
            def ep(mo, t0, t1, ps, pool_, bank):
                p.wait(nc.scalar, "mm")
                s = nc.scalar.activation(dst[:, mo * t_blk + t0: mo * t_blk + t1], ps,
                                         AF.Gelu, scale=scale_tile[:, mo:mo + 1])
                pool_.release(bank, semname, p.inc(s, semname, 1))
            return ep

        gemm(p, f"{tag}h1", lhsT_dram=W["h1T"], K=DH, M=DH, T=wb,
             rhs=H_rhs, epilogue=epG(W["h1s"], H2, "ffG"), pool=pool, ws=ws,
             t_tile=t_blk, rhs_ready=("ffB", p.val("ffB")))
        gemm(p, f"{tag}h2", lhsT_dram=W["h2T"], K=DH, M=DH, T=wb,
             rhs=H2_rhs, epilogue=epG(W["h2s"], H, "ffG"), pool=pool, ws=ws,
             t_tile=t_blk, rhs_ready=("ffG", p.val("ffG")))

        def epO(mo, t0, t1, ps, pool_, bank):
            p.wait(nc.vector, "mm")
            v = nc.vector.scalar_tensor_tensor(yout[:, mo * t_blk + t0: mo * t_blk + t1],
                                               ps, W["w3s"][:, mo:mo + 1],
                                               xf[:, mo * t_blk + t0: mo * t_blk + t1],
                                               ALU.mult, ALU.add)
            pool_.release(bank, "ffO", p.inc(v, "ffO", 1))

        gemm(p, f"{tag}w3", lhsT_dram=W["w3T"], K=DH, M=D, T=wb,
             rhs=H_rhs, epilogue=epO, pool=pool, ws=ws,
             t_tile=t_blk, rhs_ready=("ffG", p.val("ffG")))
        p.wait(nc.sync, "ffO")
        for j in range(NCH):
            p.inc(nc.sync.dma_start(out=y_dram[j * 128:(j + 1) * 128, b0:b1],
                                    in_=yout[:, j * t_blk: j * t_blk + wb]), "yst", 16)
    p.wait(nc.sync, "yst")


def stage_l1ffn_weights(p, d, prefix="L1_ffn"):
    W = {}
    W["w1T"] = p.din(f"{prefix}_w1T", f32c(np.asarray(d["L1_ffn_w1_w"]).T))
    W["w2T"] = p.din(f"{prefix}_w2T", f32c(np.asarray(d["L1_ffn_w2_w"]).T))
    W["h1T"] = p.din(f"{prefix}_h1T", f32c(np.asarray(d["L1_ffn_h1_w"]).T))
    W["h2T"] = p.din(f"{prefix}_h2T", f32c(np.asarray(d["L1_ffn_h2_w"]).T))
    W["w3T"] = p.din(f"{prefix}_w3T", f32c(np.asarray(d["L1_ffn_w3_w"]).T))
    scales = {"w1s": (np.asarray(d["L1_ffn_w1_s"]), 32),
              "w2s": (np.asarray(d["L1_ffn_w2_s"]), 32),
              "h1s": (np.asarray(d["L1_ffn_h1_s"]), 32),
              "h2s": (np.asarray(d["L1_ffn_h2_s"]), 32),
              "w3s": (np.asarray(d["L1_ffn_w3_s"]), 8)}
    for nm, (vec, n) in scales.items():
        dr = p.din(f"{prefix}_{nm}_d", chunk_vec(vec))
        t = p.sbuf(f"{prefix}_{nm}_t", [128, n], F32)
        p.inc(p.nc.sync.dma_start(out=t[:, :], in_=dr[:, :]), "cld", 16)
        W[nm] = t
    return W


EPS_COLS = {1e-5: 0, 1e-8: 1, 1e-12: 2, 1.0: 3, 0.0: 4}

def stage_consts(p):
    ones = p.din("c_ones", np.ones((128, 128), np.float32))
    t = p.sbuf("c_ones_t", [128, 128], F32)
    p.inc(p.nc.sync.dma_start(out=t[:, :], in_=ones[:, :]), "cld", 16)
    ev = np.zeros((128, 8), np.float32)
    for val, col in EPS_COLS.items():
        ev[:, col] = val
    ed = p.din("c_eps", ev)
    et = p.sbuf("c_eps_t", [128, 8], F32)
    p.inc(p.nc.sync.dma_start(out=et[:, :], in_=ed[:, :]), "cld", 16)
    return {"ones_f32": t, "eps": et}


def shard_x(x):
    out = []
    for c in range(NCORE):
        b, q = c // 4, c % 4
        out.append(f32c(np.asarray(x)[b, q * TC:(q + 1) * TC, :].T))
    return out


def unshard_y(parts):
    y = np.empty((BATCH, SEQ, D), np.float32)
    for c in range(NCORE):
        b, q = c // 4, c % 4
        y[b, q * TC:(q + 1) * TC, :] = parts[c].T
    return y


def run_l1ffn(x, d):
    p = Prog()
    x_dram = p.din("x", shard_x(x))
    y_dram = p.dout("y", [D, TC])
    consts = stage_consts(p)
    W = stage_l1ffn_weights(p, d)
    for e in (p.nc.tensor, p.nc.scalar, p.nc.vector):
        p.wait(e, "cld")
    emit_l1ffn(p, "f", x_dram=x_dram, y_dram=y_dram, W=W, consts=consts)
    res = p.run()
    p.close()
    return unshard_y([r["y"] for r in res.results])


def _run_l1ffn_parts(x_parts, d):
    p = Prog()
    x_dram = p.din("x", x_parts)
    y_dram = p.dout("y", [D, TC])
    consts = stage_consts(p)
    W = stage_l1ffn_weights(p, d)
    for e in (p.nc.tensor, p.nc.scalar, p.nc.vector):
        p.wait(e, "cld")
    emit_l1ffn(p, "f", x_dram=x_dram, y_dram=y_dram, W=W, consts=consts)
    res = p.run()
    p.close()
    return [r["y"] for r in res.results]


def kernel(**inputs):
    d = inputs
    x = np.asarray(d["L0_attn_a_x"])
    ph1 = run_mamba_a(x, d)
    yp = run_mamba_b(ph1, d)
    yp = run_l0ffn_app(yp, d)
    qn, mp = run_l0attnb_a(yp, d)
    yp = run_l0attnb_b(yp, qn, mp, d)
    yp = run_l0ffn_app(yp, d)
    u, v, xr = run_l1attna_a(yp, d)
    yp = run_l1attna_b(xr, u, v, d)
    yp = _run_l1ffn_parts(yp, d)
    qa, ka, vt = run_l1attnb_a(yp, d)
    yp = run_l1attnb_b(yp, qa, ka, vt, d)
    yp = _run_l1ffn_parts(yp, d)
    return unshard_y(yp)



HEADS_B = 16
HD_B = 64


def stage_ind_consts(p):
    ind = np.zeros((128, NCH * HEADS_B), np.float32)
    indT = np.zeros((16, NCH * 128), np.float32)
    for j in range(NCH):
        for pp in range(128):
            h = (j * 128 + pp) // HD_B
            ind[pp, j * HEADS_B + h] = 1.0
            indT[h, j * 128 + pp] = 1.0
    di = p.din("c_ind", ind)
    ti = p.sbuf("c_ind_t", [128, NCH * HEADS_B], F32)
    p.inc(p.nc.sync.dma_start(out=ti[:, :], in_=di[:, :]), "cld", 16)
    dt_ = p.din("c_indT", indT)
    tt = p.sbuf("c_indT_t", [16, NCH * 128], F32)
    p.inc(p.nc.sync.dma_start(out=tt[:, :], in_=dt_[:, :]), "cld", 16)
    return {"ind": ti, "indT": tt}


def load_resident_bf16(p, name, arr2d):
    K, M = arr2d.shape
    nk = K // 128
    dr = p.din(name + "_d", f32c(arr2d))
    t = p.sbuf(name + "_t", [128, nk * M], BF16)
    src = dr[:, :].rearrange("(kc kp) m -> kp kc m", kp=128)
    dst = t[:, :].rearrange("p (kc m) -> p kc m", m=M)
    p.inc(p.nc.gpsimd.dma_start(out=dst, in_=src), "cld", 16)
    return t


def emit_ln_from_dram(p, tag, *, x_dram, xn_tile, T, consts, pool, scr, t_blk=512,
                      xf_tile=None, gain=None, gbias=None):
    nc = p.nc
    u = p.uid()
    xbuf = p.sbuf(f"lnx{u}", [128, NCH * t_blk], F32)
    nblk = (T + t_blk - 1) // t_blk
    for blk in range(nblk):
        b0, b1 = blk * t_blk, min(T, (blk + 1) * t_blk)
        wb = b1 - b0
        p.wait(nc.sync, "ln_v")
        p.wait(nc.sync, "ln_s")
        p.wait(nc.sync, "mm")
        for j in range(NCH):
            p.inc(nc.sync.dma_start(out=xbuf[:, j * t_blk: j * t_blk + wb],
                                    in_=x_dram[j * 128:(j + 1) * 128, b0:b1]), "xld", 16)
        p.wait(nc.scalar, "xld")
        p.wait(nc.tensor, "xld")
        p.wait(nc.vector, "xld")
        layernorm(p, f"{tag}{blk}", X=lambda j, a, b: xbuf[:, j * t_blk + a: j * t_blk + b],
                  T=wb, scr=scr,
                  out_bf=None if xn_tile is None else
                      (lambda j, a, b: xn_tile[:, j * T + b0 + a: j * T + b0 + b]),
                  out_f32=None if xf_tile is None else
                      (lambda j, a, b: xf_tile[:, j * T + b0 + a: j * T + b0 + b]),
                  gain=gain, gbias=gbias,
                  pool=pool, ones_f32=consts["ones_f32"], eps_ap=consts["eps"][0:1, 0:1])


def run_l0attnb_a(x_parts, d):
    p = Prog()
    nc = p.nc
    x_dram = p.din("x", x_parts)
    qn_out = p.dout("qn", [D, TC], BF16)
    mp_out = p.dout("mp", [HD_B, D])
    consts = stage_consts(p)
    ic = stage_ind_consts(p)
    WQ = load_resident_bf16(p, "WQ", np.asarray(d["L0_attn_b_Wq"]))
    WK = load_resident_bf16(p, "WK", np.asarray(d["L0_attn_b_Wk"]))
    WV = load_resident_bf16(p, "WV", np.asarray(d["L0_attn_b_Wv"]))
    bq_d = p.din("bq", chunk_vec(np.asarray(d["L0_attn_b_bq"])))
    bq_t = p.sbuf("bq_t", [128, NCH], F32)
    p.inc(nc.sync.dma_start(out=bq_t[:, :], in_=bq_d[:, :]), "cld", 16)
    bkv_d = p.din("bkv", f32c(np.concatenate([np.asarray(d["L0_attn_b_bk"]),
                                               np.asarray(d["L0_attn_b_bv"])]).reshape(1, 2 * D)))
    bkv_t = p.sbuf("bkv_t", [1, 2 * D], BF16)
    p.inc(nc.gpsimd.dma_start(out=bkv_t[0:1, :], in_=bkv_d[:, :]), "cld", 16)
    ones_bf = p.sbuf("ones_bf", [1, 128], BF16)
    vv0 = p.inc(nc.vector.memset(ones_bf[0:1, :], 1.0), "q_v", 1)
    pool = PsumPool(p, "pp", banks=4)
    scr = make_ln_scratch(p)
    mp_ps = p.psum("mp_ps", [64, 1024], F32)
    p.join_engines(sems=["cld"])
    xn = p.sbuf("xn", [128, NCH * TC], BF16)
    emit_ln_from_dram(p, "ln3", x_dram=x_dram, xn_tile=xn, T=TC, consts=consts,
                      pool=pool, scr=scr)
    qtmp = [p.sbuf(f"qtmp{i}", [128, 512], F32) for i in range(2)]
    qnb = [p.sbuf(f"qnb{i}", [128, 512], BF16) for i in range(2)]
    sqs = p.sbuf("sqs", [128, 512], F32)
    nrow = p.sbuf("nrow", [16, 512], F32)
    nrow2 = p.sbuf("nrow2", [16, 512], F32)
    lnv = p.val("ln_v")
    nc.tensor.wait_ge(p.sem("ln_v"), lnv)
    for mo in range(NCH):
        for tt in range(TC // 512):
            t0, t1 = tt * 512, (tt + 1) * 512
            it = mo * (TC // 512) + tt
            par = it % 2
            psQ, bankQ = pool.acquire()
            for kc in range(NCH):
                mm = nc.tensor.matmul(psQ[:, :], WQ[:, kc * D + mo * 128: kc * D + (mo + 1) * 128],
                                      xn[:, kc * TC + t0: kc * TC + t1],
                                      start=(kc == 0), stop=(kc == NCH - 1))
            mmv = p.inc(mm, "mm", 1)
            nc.vector.wait_ge(p.sem("mm"), mmv)
            if it >= 2:
                p.wait(nc.vector, "qst")
                p.wait(nc.vector, "q_s")
            v = nc.vector.tensor_scalar_add(qtmp[par][:, :], psQ, bq_t[:, mo:mo + 1])
            qv = p.inc(v, "q_v", 1)
            pool.release(bankQ, "q_v", qv)
            nc.scalar.wait_ge(p.sem("q_v"), qv)
            p.wait(nc.scalar, "mm")
            s = nc.scalar.activation(sqs[:, :], qtmp[par][:, :], AF.Square)
            sv = p.inc(s, "q_s", 1)
            psN, bankN = pool.acquire()
            nc.tensor.wait_ge(p.sem("q_s"), sv)
            mm = nc.tensor.matmul(psN[0:16, :], ic["ind"][:, mo * 16:(mo + 1) * 16],
                                  sqs[:, :], start=True, stop=True)
            mmv = p.inc(mm, "mm", 1)
            nc.scalar.wait_ge(p.sem("mm"), mmv)
            s = nc.scalar.activation(nrow[:, :], psN[0:16, :], AF.Sqrt)
            sv = p.inc(s, "q_s", 1)
            pool.release(bankN, "q_s", sv)
            nc.vector.wait_ge(p.sem("q_s"), sv)
            v = nc.vector.tensor_scalar_max(nrow2[:, :], nrow[:, :], 1e-12)
            cv = p.inc(v, "q_v", 1)
            nc.scalar.wait_ge(p.sem("q_v"), cv)
            s = nc.scalar.activation(nrow[:, :], nrow2[:, :], AF.Ln)
            p.inc(s, "q_s", 1)
            s = nc.scalar.activation(nrow[:, :], nrow[:, :], AF.Exp, scale=-1.0)
            rv = p.inc(s, "q_s", 1)
            psR, bankR = pool.acquire()
            nc.tensor.wait_ge(p.sem("q_s"), rv)
            mm = nc.tensor.matmul(psR[:, :], ic["indT"][:, mo * 128:(mo + 1) * 128],
                                  nrow[:, :], start=True, stop=True)
            mmv = p.inc(mm, "mm", 1)
            nc.vector.wait_ge(p.sem("mm"), mmv)
            v = nc.vector.tensor_tensor(qnb[par][:, :], qtmp[par][:, :], psR, ALU.mult)
            qv = p.inc(v, "q_v", 1)
            pool.release(bankR, "q_v", qv)
            nc.sync.wait_ge(p.sem("q_v"), qv)
            p.inc(nc.sync.dma_start(out=qn_out[mo * 128:(mo + 1) * 128, t0:t1],
                                    in_=qnb[par][:, :]), "qst", 16)
    ktile = p.sbuf("ktile", [128, 1024], BF16)
    ksq = p.sbuf("ksq", [128, 1024], F32)
    vtile = p.sbuf("vtile", [128, 1024], BF16)
    knsq = p.sbuf("knsq", [128, 16], F32)
    krec = p.sbuf("krec", [128, 16], F32)
    krec2 = p.sbuf("krec2", [128, 16], F32)
    NTCH = TC // 128
    for c in range(NTCH):
        tt0 = c * 128
        p.wait(nc.vector, "mm")
        p.wait(nc.scalar, "mm")
        kb = []
        for half in range(2):
            psK, bankK = pool.acquire()
            kb.append((psK, bankK))
            for kc in range(NCH):
                nc.tensor.matmul(psK[:, :], xn[:, kc * TC + tt0: kc * TC + tt0 + 128],
                                WK[:, kc * D + half * 512: kc * D + (half + 1) * 512],
                                start=(kc == 0), stop=False)
            mm = nc.tensor.matmul(psK[:, :], ones_bf[0:1, :],
                                  bkv_t[0:1, half * 512:(half + 1) * 512],
                                  start=False, stop=True)
            mmv = p.inc(mm, "mm", 1)
            nc.scalar.wait_ge(p.sem("mm"), mmv)
            p.wait(nc.scalar, "kv_v")
            s = nc.scalar.activation(ksq[:, half * 512:(half + 1) * 512], psK, AF.Square)
            sqv = p.inc(s, "kv_s", 1)
            nc.vector.wait_ge(p.sem("kv_s"), sqv)
            for h8 in range(8):
                v = nc.vector.reduce_sum(knsq[:, half * 8 + h8: half * 8 + h8 + 1],
                                         ksq[:, half * 512 + h8 * 64: half * 512 + (h8 + 1) * 64],
                                         axis=AX.X)
                p.inc(v, "kv_v", 1)
        kvv = p.val("kv_v")
        nc.scalar.wait_ge(p.sem("kv_v"), kvv)
        s = nc.scalar.activation(krec[:, :], knsq[:, :], AF.Sqrt)
        sv = p.inc(s, "kv_s", 1)
        nc.vector.wait_ge(p.sem("kv_s"), sv)
        v = nc.vector.tensor_scalar_max(krec2[:, :], krec[:, :], 1e-12)
        cv = p.inc(v, "kv_v", 1)
        nc.scalar.wait_ge(p.sem("kv_v"), cv)
        s = nc.scalar.activation(krec[:, :], krec2[:, :], AF.Ln)
        p.inc(s, "kv_s", 1)
        s = nc.scalar.activation(krec[:, :], krec[:, :], AF.Exp, scale=-1.0)
        p.inc(s, "kv_s", 1)
        nc.vector.wait_ge(p.sem("kv_s"), p.val("kv_s"))
        for half in range(2):
            psK, bankK = kb[half]
            for h8 in range(8):
                h = half * 8 + h8
                v = nc.vector.tensor_scalar_mul(ktile[:, h * 64:(h + 1) * 64],
                                                psK[:, h8 * 64:(h8 + 1) * 64],
                                                krec[:, h:h + 1])
                kvv = p.inc(v, "kv_v", 1)
            pool.release(bankK, "kv_v", kvv)
        for half in range(2):
            psV, bankV = pool.acquire()
            for kc in range(NCH):
                nc.tensor.matmul(psV[:, :], xn[:, kc * TC + tt0: kc * TC + tt0 + 128],
                                WV[:, kc * D + half * 512: kc * D + (half + 1) * 512],
                                start=(kc == 0), stop=False)
            mm = nc.tensor.matmul(psV[:, :], ones_bf[0:1, :],
                                  bkv_t[0:1, D + half * 512: D + (half + 1) * 512],
                                  start=False, stop=True)
            mmv = p.inc(mm, "mm", 1)
            nc.scalar.wait_ge(p.sem("mm"), mmv)
            s = nc.scalar.copy(vtile[:, half * 512:(half + 1) * 512], psV[:, :])
            sv = p.inc(s, "kv_s", 1)
            pool.release(bankV, "kv_s", sv)
        nc.tensor.wait_ge(p.sem("kv_v"), p.val("kv_v"))
        nc.tensor.wait_ge(p.sem("kv_s"), p.val("kv_s"))
        for h in range(16):
            mm = nc.tensor.matmul(mp_ps[0:64, h * 64:(h + 1) * 64],
                                  ktile[:, h * 64:(h + 1) * 64],
                                  vtile[:, h * 64:(h + 1) * 64],
                                  start=(c == 0 and h % 8 == 0), stop=(c == NTCH - 1),
                                  skip_group_check=True)
        p.inc(mm, "mm", 1)
    mpf = p.sbuf("mpf", [64, 1024], F32)
    mmv = p.val("mm")
    nc.scalar.wait_ge(p.sem("mm"), mmv)
    s = nc.scalar.copy(mpf[0:64, 0:512], mp_ps[0:64, 0:512])
    p.inc(s, "kv_s", 1)
    s = nc.scalar.copy(mpf[0:64, 512:1024], mp_ps[0:64, 512:1024])
    sv = p.inc(s, "kv_s", 1)
    nc.sync.wait_ge(p.sem("kv_s"), sv)
    p.inc(nc.sync.dma_start(out=mp_out[:, :], in_=mpf[:, :]), "qst", 16)
    p.wait(nc.sync, "qst")
    res = p.run()
    p.close()
    qn_parts = [r["qn"] for r in res.results]
    mp_parts = [r["mp"] for r in res.results]
    return qn_parts, mp_parts


def run_l0attnb_b(x_parts, qn_parts, mp_parts, d):
    p = Prog()
    nc = p.nc
    x_dram = p.din("x", x_parts)
    qn_dram = p.din("qn", qn_parts, BF16)
    mall = []
    for c in range(NCORE):
        b = c // 4
        mall.append(np.stack([np.asarray(mp_parts[b * 4 + i]) for i in range(4)]))
    mall_dram = p.din("mall", mall)
    m_dram = p.din("mvec", f32c(np.asarray(d["L0_attn_b_m"]).reshape(16, 1)))
    Wo = p.din("Wo", f32c(np.asarray(d["L0_attn_b_Wo"])))
    bo_d = p.din("bo", chunk_vec(np.asarray(d["L0_attn_b_bo"])))
    bo_t = p.sbuf("bo_t", [128, NCH], F32)
    p.inc(nc.sync.dma_start(out=bo_t[:, :], in_=bo_d[:, :]), "cld", 16)
    consts = stage_consts(p)
    ic = stage_ind_consts(p)
    y_dram = p.dout("y", [D, TC])
    pool = PsumPool(p, "pp", banks=6)
    ws = WStream(p, "ws", 4096)
    mbuf = p.sbuf("mbuf", [64, 4 * D], F32)
    for a in range(4):
        p.inc(nc.sync.dma_start(out=mbuf[:, a * D:(a + 1) * D],
                                in_=mall_dram[a, :, :]), "cld", 16)
    MB = p.sbuf("MB", [128, D], BF16)
    msum = p.sbuf("msum", [64, D], F32)
    p.join_engines(sems=["cld"])
    v = nc.vector.tensor_add(msum[:, :], mbuf[0:64, 0:D], mbuf[0:64, D:2 * D])
    p.inc(v, "b_v", 1)
    v = nc.vector.tensor_add(msum[:, :], msum[:, :], mbuf[0:64, 2 * D:3 * D])
    p.inc(v, "b_v", 1)
    v = nc.vector.tensor_add(msum[:, :], msum[:, :], mbuf[0:64, 3 * D:4 * D])
    vv = p.inc(v, "b_v", 1)
    nc.scalar.wait_ge(p.sem("b_v"), vv)
    s = nc.scalar.copy(MB[0:64, :], msum[:, :])
    p.inc(s, "b_s", 1)
    s = nc.scalar.copy(MB[64:128, :], msum[:, :])
    p.inc(s, "b_s", 1)
    mrow = p.sbuf("mrow", [16, 1], F32)
    p.inc(nc.sync.dma_start(out=mrow[:, :], in_=m_dram[:, :]), "cld2", 16)
    p.wait(nc.scalar, "cld2")
    s = nc.scalar.activation(mrow[:, :], mrow[:, :], AF.Sigmoid)
    p.inc(s, "b_s", 1)
    s = nc.scalar.activation(mrow[:, :], mrow[:, :], AF.Exp, scale=float(np.log(SEQ)))
    sv = p.inc(s, "b_s", 1)
    nc.vector.wait_ge(p.sem("b_s"), sv)
    mrow2 = p.sbuf("mrow2", [16, 1], F32)
    v = nc.vector.reciprocal(mrow2[:, :], mrow[:, :])
    vv = p.inc(v, "b_v", 1)
    rsr = p.sbuf("rsr", [128, NCH], F32)
    psR, bankR = pool.acquire()
    nc.tensor.wait_ge(p.sem("b_v"), vv)
    for j in range(NCH):
        mm = nc.tensor.matmul(psR[:, j:j + 1], ic["indT"][:, j * 128:(j + 1) * 128],
                              mrow2[:, :], start=(j == 0), stop=(j == NCH - 1),
                              skip_group_check=True)
    mmv = p.inc(mm, "mm", 1)
    nc.vector.wait_ge(p.sem("mm"), mmv)
    v = nc.vector.tensor_copy(rsr[:, :], psR[:, 0:NCH])
    vv = p.inc(v, "b_v", 1)
    pool.release(bankR, "b_v", vv)
    AO = p.sbuf("AO", [128, NCH * TC], BF16)
    znb = [p.sbuf(f"znb{i}", [128, 512], BF16) for i in range(2)]
    qnl = [p.sbuf(f"qnl{i}", [128, 512], BF16) for i in range(2)]
    p.wait(nc.vector, "b_s")
    p.wait(nc.tensor, "b_s")
    for j in range(NCH):
        for tt in range(TC // 512):
            t0, t1 = tt * 512, (tt + 1) * 512
            it = j * (TC // 512) + tt
            par = it % 2
            if it >= 2:
                p.wait(nc.sync, "b_v")
                p.wait(nc.sync, "mm")
            p.inc(nc.sync.dma_start(out=qnl[par][:, :],
                                    in_=qn_dram[j * 128:(j + 1) * 128, t0:t1]), "qld", 16)
            qldv = p.val("qld")
            nc.vector.wait_ge(p.sem("qld"), qldv)
            v = nc.vector.tensor_scalar_mul(znb[par][:, :], qnl[par][:, :], rsr[:, j:j + 1])
            znv = p.inc(v, "b_v", 1)
            psA, bankA = pool.acquire()
            nc.tensor.wait_ge(p.sem("b_v"), znv)
            for hh in range(2):
                h = 2 * j + hh
                mm = nc.tensor.matmul(psA[hh * 64:(hh + 1) * 64, :],
                                      MB[hh * 64:(hh + 1) * 64 - 0, h * 64:(h + 1) * 64],
                                      znb[par][hh * 64:(hh + 1) * 64, :],
                                      start=True, stop=True)
            mmv = p.inc(mm, "mm", 1)
            nc.scalar.wait_ge(p.sem("mm"), mmv)
            s = nc.scalar.copy(AO[:, j * TC + t0: j * TC + t1], psA[:, :])
            sv = p.inc(s, "b_s", 1)
            pool.release(bankA, "b_s", sv)
    xf = p.sbuf("xf", [128, NCH * TC], F32)
    for j in range(NCH):
        p.inc(nc.sync.dma_start(out=xf[:, j * TC:(j + 1) * TC],
                                in_=x_dram[j * 128:(j + 1) * 128, :]), "xld", 16)
    yout = [p.sbuf(f"yo{i}", [128, 512], F32) for i in range(4)]
    p.wait(nc.vector, "xld")
    bsv = p.val("b_s")

    def epO(mo, t0, t1, ps, pool_, bank):
        it = mo * (TC // 512) + t0 // 512
        par = it % 4
        p.wait(nc.vector, "mm")
        if it >= 4:
            p.wait(nc.vector, "yst")
        v = nc.vector.scalar_tensor_tensor(yout[par][:, 0:t1 - t0], ps, bo_t[:, mo:mo + 1],
                                           xf[:, mo * TC + t0: mo * TC + t1],
                                           ALU.add, ALU.add)
        yv = p.inc(v, "b_v", 1)
        pool_.release(bank, "b_v", yv)
        nc.sync.wait_ge(p.sem("b_v"), yv)
        p.inc(nc.sync.dma_start(out=y_dram[mo * 128:(mo + 1) * 128, t0:t1],
                                in_=yout[par][:, 0:t1 - t0]), "yst", 16)

    AO_rhs = lambda kc, a, b: AO[0:128, kc * TC + a: kc * TC + b]
    gemm(p, "wo", lhsT_dram=Wo, K=D, M=D, T=TC, rhs=AO_rhs, epilogue=epO,
         pool=pool, ws=ws, rhs_ready=("b_s", bsv))
    p.wait(nc.sync, "yst")
    res = p.run()
    p.close()
    return [r["y"] for r in res.results]

def emit_qk_aug(p, tag, *, W_res, xn, out_dram, ic, pool, T=TC):
    nc = p.nc
    u = p.uid()
    qb = [p.sbuf(f"qk{u}_{i}", [128, 512], BF16) for i in range(2)]
    sqs = p.sbuf(f"qksq{u}", [128, 512], F32)
    nq = p.sbuf(f"qknq{u}", [16, 512], F32)
    nqb = p.sbuf(f"qknqb{u}", [16, 512], BF16)
    for mo in range(NCH):
        for tt in range(T // 512):
            t0, t1 = tt * 512, (tt + 1) * 512
            it = mo * (T // 512) + tt
            par = it % 2
            psQ, bankQ = pool.acquire()
            for kc in range(NCH):
                mm = nc.tensor.matmul(psQ[:, :], W_res[:, kc * D + mo * 128: kc * D + (mo + 1) * 128],
                                      xn[:, kc * TC + t0: kc * TC + t1],
                                      start=(kc == 0), stop=(kc == NCH - 1))
            mmv = p.inc(mm, "mm", 1)
            nc.scalar.wait_ge(p.sem("mm"), mmv)
            if it >= 2:
                p.wait(nc.scalar, "qkst")
            s = nc.scalar.copy(qb[par][:, :], psQ)
            p.inc(s, "qk_s", 1)
            p.wait(nc.scalar, "mm")
            s = nc.scalar.activation(sqs[:, :], psQ, AF.Square)
            sv = p.inc(s, "qk_s", 1)
            pool.release(bankQ, "qk_s", sv)
            psN, bankN = pool.acquire()
            nc.tensor.wait_ge(p.sem("qk_s"), sv)
            mm = nc.tensor.matmul(psN[0:16, :], ic["ind"][:, mo * 16:(mo + 1) * 16],
                                  sqs[:, :], start=True, stop=True)
            mmv = p.inc(mm, "mm", 1)
            nc.scalar.wait_ge(p.sem("mm"), mmv)
            p.wait(nc.scalar, "qkst")
            s = nc.scalar.activation(nq[:, :], psN[0:16, :], AF.Sqrt)
            p.inc(s, "qk_s", 1)
            s = nc.scalar.copy(nqb[:, :], nq[:, :])
            sv = p.inc(s, "qk_s", 1)
            pool.release(bankN, "qk_s", sv)
            nc.sync.wait_ge(p.sem("qk_s"), sv)
            for hh in range(2):
                p.inc(nc.sync.dma_start(
                    out=out_dram[2 * mo + hh, 0:64, t0:t1],
                    in_=qb[par][hh * 64:(hh + 1) * 64, :]), "qkst", 16)
            p.inc(nc.sync.dma_start(
                out=out_dram[2 * mo: 2 * mo + 2, 64, t0:t1],
                in_=nqb[2 * mo: 2 * mo + 2, :]), "qkst", 16)


def run_l1attnb_a(x_parts, d):
    p = Prog()
    nc = p.nc
    x_dram = p.din("x", x_parts)
    qa_out = p.dout("qa", [16, 65, TC], BF16)
    ka_out = p.dout("ka", [16, 65, TC], BF16)
    vt_out = p.dout("vt", [TC, 16 * 65], BF16)
    consts = stage_consts(p)
    ic = stage_ind_consts(p)
    Wfull = np.asarray(d["L1_attn_b_W_qkv"])
    WQ = load_resident_bf16(p, "WQ", f32c(Wfull[0:D].T))
    WK = load_resident_bf16(p, "WK", f32c(Wfull[D:2 * D].T))
    WV = load_resident_bf16(p, "WV", f32c(Wfull[2 * D:3 * D].T))
    pool = PsumPool(p, "pp", banks=6)
    scr = make_ln_scratch(p)
    p.join_engines(sems=["cld"])
    xn = p.sbuf("xn", [128, NCH * TC], BF16)
    emit_ln_from_dram(p, "ln7", x_dram=x_dram, xn_tile=xn, T=TC, consts=consts,
                      pool=pool, scr=scr)
    nc.tensor.wait_ge(p.sem("ln_v"), p.val("ln_v"))
    emit_qk_aug(p, "q7", W_res=WQ, xn=xn, out_dram=qa_out, ic=ic, pool=pool)
    emit_qk_aug(p, "k7", W_res=WK, xn=xn, out_dram=ka_out, ic=ic, pool=pool)
    vts = p.sbuf("vts", [128, 16 * 65], BF16)
    vband = p.sbuf("vband", [128, 1024], BF16)
    v0 = nc.vector.memset(vts[:, :], 1.0)
    p.inc(v0, "kv_v", 1)
    p.wait(nc.tensor, "kv_v")
    for c in range(TC // 128):
        tt0 = c * 128
        for half in range(2):
            psV, bankV = pool.acquire()
            for kc in range(NCH):
                mm = nc.tensor.matmul(psV[:, :], xn[:, kc * TC + tt0: kc * TC + tt0 + 128],
                                      WV[:, kc * D + half * 512: kc * D + (half + 1) * 512],
                                      start=(kc == 0), stop=(kc == NCH - 1))
            mmv = p.inc(mm, "mm", 1)
            nc.scalar.wait_ge(p.sem("mm"), mmv)
            if c > 0:
                p.wait(nc.scalar, "kv_v")
            s = nc.scalar.copy(vband[:, half * 512:(half + 1) * 512], psV)
            sv = p.inc(s, "kv_s", 1)
            pool.release(bankV, "kv_s", sv)
        vsv = p.val("kv_s")
        nc.vector.wait_ge(p.sem("kv_s"), vsv)
        p.wait(nc.vector, "vst")
        for h in range(16):
            v = nc.vector.tensor_copy(vts[:, h * 65: h * 65 + 64],
                                      vband[:, h * 64:(h + 1) * 64])
            p.inc(v, "kv_v", 1)
        kvv = p.val("kv_v")
        nc.sync.wait_ge(p.sem("kv_v"), kvv)
        p.inc(nc.sync.dma_start(out=vt_out[tt0: tt0 + 128, :], in_=vts[:, :]), "vst", 16)
    p.wait(nc.sync, "qkst")
    p.wait(nc.sync, "vst")
    res = p.run()
    p.close()
    return ([r["qa"] for r in res.results], [r["ka"] for r in res.results],
            [r["vt"] for r in res.results])


def run_l1attnb_b(x_parts, qa_parts, ka_parts, vt_parts, d):
    p = Prog()
    nc = p.nc
    x_dram = p.din("x", x_parts)
    qa_dram = p.din("qa", qa_parts, BF16)
    kg, vg = [], []
    for c in range(NCORE):
        b = c // 4
        kg.append(np.concatenate([np.asarray(ka_parts[b * 4 + i]) for i in range(4)], axis=2))
        vg.append(np.concatenate([np.asarray(vt_parts[b * 4 + i]) for i in range(4)], axis=0))
    kg_dram = p.din("kg", kg, BF16)
    vg_dram = p.din("vg", vg, BF16)
    y_dram = p.dout("y", [D, TC])
    consts = stage_consts(p)
    pool = PsumPool(p, "pp", banks=4)
    psO_t = p.psum("psO", [65, 512], F32)
    psR_t = p.psum("psR", [64, 512], F32)
    ones_bf = p.sbuf("ones_bf", [1, 64], BF16)
    p.inc(nc.vector.memset(ones_bf[0:1, :], 1.0), "b_v", 1)
    KG = [p.sbuf(f"KG{i}", [65, SEQ], BF16) for i in range(2)]
    VG = [p.sbuf(f"VG{i}", [128, 64 * 65], BF16) for i in range(2)]
    QT = [p.sbuf(f"QT{i}", [65, 512], BF16) for i in range(2)]
    ES = [p.sbuf(f"ES{i}", [128, 512], BF16) for i in range(2)]
    kmrow = p.sbuf("kmrow", [1, 16], F32)
    kmneg = p.sbuf("kmneg", [1, 16], F32)
    r64 = p.sbuf("r64", [1, 512], BF16)
    drow = p.sbuf("drow", [1, 512], F32)
    drow_b = p.sbuf("drow_b", [1, 512], BF16)
    OV = p.sbuf("OV", [64, 512], F32)
    AOf = p.sbuf("AOf", [128, NCH * TC], F32)
    p.join_engines(sems=["cld"])
    ktok = [None, None]
    for h in range(16):
        par = h % 2
        if ktok[par] is not None:
            nc.sync.wait_ge(p.sem("mm"), ktok[par])
        p.inc(nc.sync.dma_start(out=KG[par][:, :], in_=kg_dram[h, :, :]), "kld", 16)
        p.inc(nc.sync.dma_start(
            out=VG[par][:, :].rearrange("p (tc c) -> p tc c", c=65),
            in_=vg_dram[:, h * 65:(h + 1) * 65].rearrange("(tc tp) c -> tp tc c", tp=128)),
            "kld", 16)
        kldv = p.val("kld")
        nc.vector.wait_ge(p.sem("kld"), kldv)
        v = nc.vector.reduce_max(kmrow[0:1, h:h + 1], KG[par][64:65, :], axis=AX.X)
        vv = p.inc(v, "b_v", 1)
        nc.scalar.wait_ge(p.sem("b_v"), vv)
        s = nc.scalar.mul(kmneg[0:1, h:h + 1], kmrow[0:1, h:h + 1], -1.0)
        smv = p.inc(s, "b_s", 1)
        v = nc.vector.memset(KG[par][64:65, :], 1.0)
        kgv = p.inc(v, "b_v", 1)
        nc.tensor.wait_ge(p.sem("b_v"), kgv)
        for qt in range(TC // 512):
            t0, t1 = qt * 512, (qt + 1) * 512
            iq = h * 4 + qt
            qpar = iq % 2
            if iq >= 2:
                nc.sync.wait_ge(p.sem("mm"), p.val("mm"))
            p.inc(nc.sync.dma_start(out=QT[qpar][:, :], in_=qa_dram[h, :, t0:t1]), "qld", 16)
            qldv = p.val("qld")
            nc.scalar.wait_ge(p.sem("qld"), qldv)
            nc.scalar.wait_ge(p.sem("b_v"), p.val("b_v"))
            s = nc.scalar.copy(r64[0:1, :], QT[qpar][64:65, :])
            sv = p.inc(s, "b_s", 1)
            nc.vector.wait_ge(p.sem("b_s"), sv)
            v = nc.vector.tensor_scalar_mul(r64[0:1, :], r64[0:1, :], kmneg[0:1, h:h + 1])
            p.inc(v, "b_v", 1)
            v = nc.vector.tensor_copy(QT[qpar][64:65, :], r64[0:1, :])
            qfv = p.inc(v, "b_v", 1)
            nc.tensor.wait_ge(p.sem("b_v"), qfv)
            NKC = SEQ // 128
            es_tok = [0, 0]
            evs = {}
            for kc in range(NKC + 1):
                if kc < NKC:
                    psS, bankS = pool.acquire()
                    mm = nc.tensor.matmul(psS[:, :], KG[par][:, kc * 128:(kc + 1) * 128],
                                          QT[qpar][:, :], start=True, stop=True)
                    mmv = p.inc(mm, "mm", 1)
                    nc.scalar.wait_ge(p.sem("mm"), mmv)
                    epar = kc % 2
                    if es_tok[epar]:
                        nc.scalar.wait_ge(p.sem("mm"), es_tok[epar])
                    s = nc.scalar.activation(ES[epar][:, :], psS, AF.Exp, scale=0.125)
                    evs[kc] = sv = p.inc(s, "b_s", 1)
                    pool.release(bankS, "b_s", sv)
                if kc >= 1:
                    kcm = kc - 1
                    epar = kcm % 2
                    nc.tensor.wait_ge(p.sem("b_s"), evs[kcm])
                    mm = nc.tensor.matmul(psO_t[:, :], VG[par][:, kcm * 65:(kcm + 1) * 65],
                                          ES[epar][:, :], start=(kcm == 0), stop=(kcm == NKC - 1))
                    es_tok[epar] = p.inc(mm, "mm", 1)
            mmv = p.val("mm")
            nc.vector.wait_ge(p.sem("mm"), mmv)
            v = nc.vector.reciprocal(drow[0:1, :], psO_t[64:65, :])
            rv = p.inc(v, "b_v", 1)
            nc.scalar.wait_ge(p.sem("b_v"), rv)
            s = nc.scalar.copy(drow_b[0:1, :], drow[0:1, :])
            p.inc(s, "b_s", 1)
            s = nc.scalar.copy(OV[:, :], psO_t[0:64, :])
            sv = p.inc(s, "b_s", 1)
            nc.tensor.wait_ge(p.sem("b_s"), sv)
            nc.tensor.wait_ge(p.sem("b_v"), p.val("b_v"))
            mm = nc.tensor.matmul(psR_t[:, :], ones_bf[0:1, :], drow_b[0:1, :],
                                  start=True, stop=True)
            mv2 = p.inc(mm, "mm", 1)
            nc.vector.wait_ge(p.sem("mm"), mv2)
            base = (h % 2) * 64
            j = h // 2
            v = nc.vector.tensor_tensor(AOf[base:base + 64, j * TC + t0: j * TC + t1],
                                        OV[:, :], psR_t[:, :], ALU.mult)
            p.inc(v, "b_v", 1)
        ktok[par] = p.val("mm")
    xin = [p.sbuf(f"xi{i}", [128, 512], F32) for i in range(2)]
    yo = [p.sbuf(f"yo{i}", [128, 512], F32) for i in range(2)]
    aov = p.val("b_v")
    for j in range(NCH):
        for tt in range(TC // 512):
            t0, t1 = tt * 512, (tt + 1) * 512
            it = j * (TC // 512) + tt
            par = it % 2
            if it >= 2:
                p.wait(nc.sync, "y_v")
            p.inc(nc.sync.dma_start(out=xin[par][:, :],
                                    in_=x_dram[j * 128:(j + 1) * 128, t0:t1]), "xld", 16)
            xldv = p.val("xld")
            nc.vector.wait_ge(p.sem("xld"), xldv)
            if it == 0:
                nc.vector.wait_ge(p.sem("b_v"), aov)
            if it >= 2:
                p.wait(nc.vector, "yst")
            v = nc.vector.tensor_add(yo[par][:, :], xin[par][:, :],
                                     AOf[:, j * TC + t0: j * TC + t1])
            yv = p.inc(v, "y_v", 1)
            nc.sync.wait_ge(p.sem("y_v"), yv)
            p.inc(nc.sync.dma_start(out=y_dram[j * 128:(j + 1) * 128, t0:t1],
                                    in_=yo[par][:, :]), "yst", 16)
    p.wait(nc.sync, "yst")
    res = p.run()
    p.close()
    return [r["y"] for r in res.results]

def run_l1attna_a(x_parts, d):
    p = Prog()
    nc = p.nc
    x_dram = p.din("x", x_parts)
    u_out = p.dout("u", [2048, TC], BF16)
    v_out = p.dout("v", [2048, TC], BF16)
    xr_out = p.dout("xr", [D, TC])
    consts = stage_consts(p)
    uwT = p.din("uwT", f32c(np.asarray(d["L1_attn_a_u_w"]).T))
    vwT = p.din("vwT", f32c(np.asarray(d["L1_attn_a_v_w"]).T))
    ub_d = p.din("ub", chunk_vec(np.asarray(d["L1_attn_a_u_b"])))
    vb_d = p.din("vb", chunk_vec(np.asarray(d["L1_attn_a_v_b"])))
    ub_t = p.sbuf("ub_t", [128, 16], F32)
    vb_t = p.sbuf("vb_t", [128, 16], F32)
    p.inc(nc.sync.dma_start(out=ub_t[:, :], in_=ub_d[:, :]), "cld", 16)
    p.inc(nc.sync.dma_start(out=vb_t[:, :], in_=vb_d[:, :]), "cld", 16)
    pool = PsumPool(p, "pp", banks=5)
    scr = make_ln_scratch(p)
    ws = WStream(p, "ws", 4096)
    p.join_engines(sems=["cld"])
    qln = p.sbuf("qln", [128, NCH * TC], F32)
    emit_ln_from_dram(p, "ln5", x_dram=x_dram, xn_tile=None, xf_tile=qln, T=TC,
                      consts=consts, pool=pool, scr=scr)
    xi = [p.sbuf(f"xi{i}", [128, 512], F32) for i in range(2)]
    xo = [p.sbuf(f"xo{i}", [128, 512], F32) for i in range(2)]
    p.wait(nc.vector, "ln_v")
    for j in range(NCH):
        for tt in range(TC // 512):
            t0, t1 = tt * 512, (tt + 1) * 512
            it = j * 4 + tt
            par = it % 2
            if it >= 2:
                p.wait(nc.sync, "xr_v")
            p.inc(nc.sync.dma_start(out=xi[par][:, :],
                                    in_=x_dram[j * 128:(j + 1) * 128, t0:t1]), "xld2", 16)
            nc.vector.wait_ge(p.sem("xld2"), p.val("xld2"))
            if it >= 2:
                p.wait(nc.vector, "xrst")
            v = nc.vector.tensor_add(xo[par][:, :], xi[par][:, :],
                                     qln[:, j * TC + t0: j * TC + t1])
            xv = p.inc(v, "xr_v", 1)
            nc.sync.wait_ge(p.sem("xr_v"), xv)
            p.inc(nc.sync.dma_start(out=xr_out[j * 128:(j + 1) * 128, t0:t1],
                                    in_=xo[par][:, :]), "xrst", 16)
    xq = p.sbuf("xq", [128, NCH * TC], BF16)
    sqs = p.sbuf("sqs5", [128, 512], F32)
    rrow = p.sbuf("rrow5", [1, 512], F32)
    stat = p.psum("stat5", [1, 512], F32)
    for tt in range(TC // 512):
        t0, t1 = tt * 512, (tt + 1) * 512
        for j in range(NCH):
            p.wait(nc.scalar, "mm")
            s = nc.scalar.activation(sqs[:, :], qln[:, j * TC + t0: j * TC + t1], AF.Square)
            sv = p.inc(s, "rm_s", 1)
            nc.tensor.wait_ge(p.sem("rm_s"), sv)
            mm = nc.tensor.matmul(stat[0:1, :], consts["ones_f32"][:, 0:1],
                                  sqs[:, :], start=(j == 0), stop=(j == NCH - 1))
            p.inc(mm, "mm", 1)
        mmv = p.val("mm")
        nc.scalar.wait_ge(p.sem("mm"), mmv)
        p.wait(nc.scalar, "rm_v")
        s = nc.scalar.activation(rrow[0:1, :], stat[0:1, :], AF.Sqrt, scale=1.0 / D)
        s = nc.scalar.activation(rrow[0:1, :], rrow[0:1, :], AF.Copy, bias=1e-8)
        sv = p.inc(s, "rm_s", 1)
        nc.vector.wait_ge(p.sem("rm_s"), sv)
        v = nc.vector.reciprocal(rrow[0:1, :], rrow[0:1, :])
        rv = p.inc(v, "rm_v", 1)
        psR, bankR = pool.acquire()
        nc.tensor.wait_ge(p.sem("rm_v"), rv)
        mm = nc.tensor.matmul(psR[:, :], consts["ones_f32"][0:1, :], rrow[0:1, :],
                              start=True, stop=True)
        mmv = p.inc(mm, "mm", 1)
        nc.vector.wait_ge(p.sem("mm"), mmv)
        for j in range(NCH):
            v = nc.vector.tensor_tensor(xq[:, j * TC + t0: j * TC + t1],
                                        qln[:, j * TC + t0: j * TC + t1], psR, ALU.mult)
            p.inc(v, "rm_v", 1)
        pool.release(bankR, "rm_v", p.val("rm_v"))
    xq_rhs = lambda kc, a, b: xq[0:128, kc * TC + a: kc * TC + b]
    rmv = ("rm_v", p.val("rm_v"))
    ob = [p.sbuf(f"ob{i}", [128, 512], BF16) for i in range(4)]
    def epUV(bias_t, out_dram, semname):
        def ep(mo, t0, t1, ps, pool_, bank):
            it = mo * 4 + t0 // 512
            par = it % 4
            p.wait(nc.scalar, "mm")
            if it >= 4:
                p.wait(nc.scalar, semname + "st")
            s = nc.scalar.activation(ob[par][:, :], ps, AF.Silu, bias=bias_t[:, mo:mo + 1])
            sv = p.inc(s, semname, 1)
            pool_.release(bank, semname, sv)
            nc.sync.wait_ge(p.sem(semname), sv)
            p.inc(nc.sync.dma_start(out=out_dram[mo * 128:(mo + 1) * 128, t0:t1],
                                    in_=ob[par][:, :]), semname + "st", 16)
        return ep
    gemm(p, "uw", lhsT_dram=uwT, K=D, M=2048, T=TC, rhs=xq_rhs,
         epilogue=epUV(ub_t, u_out, "u_s"), pool=pool, ws=ws, rhs_ready=rmv)
    gemm(p, "vw", lhsT_dram=vwT, K=D, M=2048, T=TC, rhs=xq_rhs,
         epilogue=epUV(vb_t, v_out, "v_s"), pool=pool, ws=ws, rhs_ready=rmv)
    p.wait(nc.sync, "u_sst")
    p.wait(nc.sync, "v_sst")
    p.wait(nc.sync, "xrst")
    res = p.run()
    p.close()
    return ([r["u"] for r in res.results], [r["v"] for r in res.results],
            [r["xr"] for r in res.results])


def run_l1attna_b(xr_parts, u_parts, v_parts, d):
    p = Prog()
    nc = p.nc
    xr_dram = p.din("xr", xr_parts)
    u_dram = p.din("u", u_parts, BF16)
    vo_dram = p.din("vo", v_parts, BF16)
    vs_parts = [v_parts[(1 - c // 4) * 4 + (c % 4)] for c in range(NCORE)]
    vs_dram = p.din("vs", vs_parts, BF16)
    zero0 = np.asarray(d["L1_attn_a_zero"])[:, 0]
    pos0 = np.asarray(d["L1_attn_a_pos"])[:, 0]
    neg0 = np.asarray(d["L1_attn_a_neg"])[:, 0]
    cA = chunk_vec(np.repeat(zero0, 256))
    cB_b0 = chunk_vec(np.repeat(neg0, 256))
    cB_b1 = chunk_vec(np.repeat(pos0, 256))
    cA_d = p.din("cA", cA)
    cB_d = p.din("cB", [cB_b0 if c // 4 == 0 else cB_b1 for c in range(NCORE)])
    cA_t = p.sbuf("cA_t", [128, 16], F32)
    cB_t = p.sbuf("cB_t", [128, 16], F32)
    p.inc(nc.sync.dma_start(out=cA_t[:, :], in_=cA_d[:, :]), "cld", 16)
    p.inc(nc.sync.dma_start(out=cB_t[:, :], in_=cB_d[:, :]), "cld", 16)
    owT = p.din("owT", f32c(np.asarray(d["L1_attn_a_o_w"]).T))
    ob_d = p.din("obv", chunk_vec(np.asarray(d["L1_attn_a_o_b"])))
    ob_t = p.sbuf("ob_t", [128, NCH], F32)
    p.inc(nc.sync.dma_start(out=ob_t[:, :], in_=ob_d[:, :]), "cld", 16)
    y_dram = p.dout("y", [D, TC])
    consts = stage_consts(p)
    pool = PsumPool(p, "pp", banks=6)
    ws = WStream(p, "ws", 4096)
    p.join_engines(sems=["cld"])
    OUT = p.sbuf("OUT", [128, 16 * TC], BF16)
    vi = [p.sbuf(f"vi{i}", [128, 3 * 512], BF16) for i in range(2)]
    t1b = p.sbuf("t1b", [128, 512], F32)
    t2b = p.sbuf("t2b", [128, 512], F32)
    for j in range(16):
        for tt in range(TC // 512):
            t0, t1 = tt * 512, (tt + 1) * 512
            it = j * 4 + tt
            par = it % 2
            if it >= 2:
                p.wait(nc.sync, "m_v")
            p.inc(nc.sync.dma_start(out=vi[par][:, 0:512],
                                    in_=vo_dram[j * 128:(j + 1) * 128, t0:t1]), "vld", 16)
            p.inc(nc.sync.dma_start(out=vi[par][:, 512:1024],
                                    in_=vs_dram[j * 128:(j + 1) * 128, t0:t1]), "vld", 16)
            p.inc(nc.sync.dma_start(out=vi[par][:, 1024:1536],
                                    in_=u_dram[j * 128:(j + 1) * 128, t0:t1]), "vld", 16)
            vldv = p.val("vld")
            nc.vector.wait_ge(p.sem("vld"), vldv)
            v = nc.vector.tensor_scalar_mul(t1b[:, :], vi[par][:, 512:1024], cB_t[:, j:j + 1])
            p.inc(v, "m_v", 1)
            v = nc.vector.scalar_tensor_tensor(t2b[:, :], vi[par][:, 0:512], cA_t[:, j:j + 1],
                                               t1b[:, :], ALU.mult, ALU.add)
            p.inc(v, "m_v", 1)
            v = nc.vector.tensor_tensor(OUT[:, j * TC + t0: j * TC + t1],
                                        t2b[:, :], vi[par][:, 1024:1536], ALU.mult)
            p.inc(v, "m_v", 1)
    xf = p.sbuf("xf", [128, NCH * TC], F32)
    for j in range(NCH):
        p.inc(nc.sync.dma_start(out=xf[:, j * TC:(j + 1) * TC],
                                in_=xr_dram[j * 128:(j + 1) * 128, :]), "xld", 16)
    p.wait(nc.vector, "xld")
    yo = [p.sbuf(f"yo{i}", [128, 512], F32) for i in range(4)]
    def epO(mo, t0, t1, ps, pool_, bank):
        it = mo * 4 + t0 // 512
        par = it % 4
        p.wait(nc.vector, "mm")
        if it >= 4:
            p.wait(nc.vector, "yst")
        v = nc.vector.scalar_tensor_tensor(yo[par][:, :], ps, ob_t[:, mo:mo + 1],
                                           xf[:, mo * TC + t0: mo * TC + t1],
                                           ALU.add, ALU.add)
        yv = p.inc(v, "y_v", 1)
        pool_.release(bank, "y_v", yv)
        nc.sync.wait_ge(p.sem("y_v"), yv)
        p.inc(nc.sync.dma_start(out=y_dram[mo * 128:(mo + 1) * 128, t0:t1],
                                in_=yo[par][:, :]), "yst", 16)
    OUT_rhs = lambda kc, a, b: OUT[0:128, kc * TC + a: kc * TC + b]
    gemm(p, "ow", lhsT_dram=owT, K=2048, M=D, T=TC, rhs=OUT_rhs, epilogue=epO,
         pool=pool, ws=ws, rhs_ready=("m_v", p.val("m_v")))
    p.wait(nc.sync, "yst")
    res = p.run()
    p.close()
    return [r["y"] for r in res.results]

D_INNER = 2048
NSTATE = 16
RANK = 64
T_BLK = 256
NBLK = TC // T_BLK
HALO = 3
T_EXT = TC + 2 * HALO


def stage_mamba_consts(p, d, tag, nc_):
    W = {}
    W["w_inT"] = p.din(f"w_inT_{tag}", f32c(np.asarray(d[f"L0_attn_a_w_in_{tag}"]).T))
    W["x_projT"] = p.din(f"x_projT_{tag}", f32c(np.asarray(d[f"L0_attn_a_x_proj_{tag}"]).T))
    W["dt_wT"] = p.din(f"dt_wT_{tag}", f32c(np.asarray(d[f"L0_attn_a_dt_w_{tag}"]).T))
    cw = np.asarray(d[f"L0_attn_a_conv_w_{tag}"])
    cwc = np.concatenate([chunk_vec(cw[:, k]) for k in range(4)], axis=1)
    W["convw_d"] = p.din(f"convw_{tag}", cwc)
    W["convb_d"] = p.din(f"convb_{tag}", chunk_vec(np.asarray(d[f"L0_attn_a_conv_b_{tag}"])))
    W["dtb_d"] = p.din(f"dtb_{tag}", chunk_vec(np.asarray(d[f"L0_attn_a_dt_b_{tag}"])))
    W["D_d"] = p.din(f"Dskip_{tag}", chunk_vec(np.asarray(d[f"L0_attn_a_D_{tag}"])))
    alog = np.asarray(d[f"L0_attn_a_A_log_{tag}"])
    ac = np.zeros((128, 16 * NSTATE), np.float32)
    for ch in range(16):
        ac[:, ch * NSTATE:(ch + 1) * NSTATE] = alog[ch * 128:(ch + 1) * 128, :]
    W["alog_d"] = p.din(f"alog_{tag}", f32c(ac))
    return W


def load_small(p, name, dram, shape, sem="cld"):
    t = p.sbuf(name, shape, F32)
    p.inc(p.nc.sync.dma_start(out=t[:, :], in_=dram[:, :]), sem, 16)
    return t


def shard_x_halo(x):
    out = []
    x = np.asarray(x)
    for c in range(NCORE):
        b, q = c // 4, c % 4
        g0 = q * TC - HALO
        buf = np.zeros((T_EXT, D), np.float32)
        s0, s1 = max(0, g0), min(SEQ, g0 + T_EXT)
        buf[s0 - g0: s1 - g0] = x[b, s0:s1]
        out.append(f32c(buf.T))
    return out


def make_mamba_bufs(p):
    u = p.uid()
    names = dict(u_blk=[128, 16 * (T_BLK + HALO)], ucsil=[128, 16 * T_BLK],
                 zsil=[128, 16 * T_BLK], delta=[128, 16 * T_BLK], ACC=[128, 16 * T_BLK],
                 bcrows=[32, T_BLK], du_t=[128, T_BLK], aarg=[128, T_BLK],
                 at=[128, T_BLK], bt=[128, T_BLK], ht=[128, T_BLK],
                 ct=[128, T_BLK], sp1=[128, T_BLK], sp2=[128, T_BLK],
                 carry=[128, 16 * NSTATE], dsum=[128, 16], dred=[128, 1],
                 smout=[128, 2 * 16 * NSTATE])
    b = {k: p.sbuf(f"mb_{k}{u}", v, F32) for k, v in names.items()}
    b["ucb"] = p.sbuf(f"mb_ucb{u}", [128, 16 * T_BLK], BF16)
    b["dtb_t"] = p.sbuf(f"mb_dtbt{u}", [64, T_BLK], BF16)
    return b


def emit_mamba_dir(p, W, Wt, *, xn, consts, pool, dirtag, delta_out, p1_out, zs_out,
                   c_out, sm_out, ws, bufs, hmask=None, dbg_uc=None):
    nc = p.nc
    u = p.uid()
    bwd = dirtag == "b"
    u_blk = bufs["u_blk"]; ucsil = bufs["ucsil"]; ucb = bufs["ucb"]
    zsil = bufs["zsil"]; delta = bufs["delta"]; ACC = bufs["ACC"]
    dtb_t = bufs["dtb_t"]; bcrows = bufs["bcrows"]; du_t = bufs["du_t"]
    aarg = bufs["aarg"]; at = bufs["at"]; bt = bufs["bt"]; ht = bufs["ht"]
    ct = bufs["ct"]; sp1 = bufs["sp1"]; sp2 = bufs["sp2"]
    carry = bufs["carry"]; dsum = bufs["dsum"]; dred = bufs["dred"]; smout = bufs["smout"]
    p.wait(nc.vector, "sm_s")
    p.wait(nc.vector, "sc_g")
    v = nc.vector.memset(carry[:, :], 0.0)
    p.inc(v, "sc_v", 1)
    v = nc.vector.memset(dsum[:, :], 0.0)
    p.inc(v, "sc_v", 1)
    TW = T_BLK + HALO
    for blk in range(NBLK):
        if not bwd:
            w0 = blk * T_BLK
        else:
            w0 = TC - (blk + 1) * T_BLK + HALO
        def epWIN(mo, t0, t1, ps, pool_, bank):
            p.wait(nc.scalar, "mm")
            p.wait(nc.scalar, "sc_v")
            p.wait(nc.scalar, "dpk")
            if mo < 16:
                dst = u_blk[:, mo * TW: (mo + 1) * TW]
                if bwd:
                    dst = dst[:, ::-1]
                s = nc.scalar.copy(dst, ps)
            else:
                ch = mo - 16
                dst = zsil[:, ch * T_BLK: (ch + 1) * T_BLK]
                zsl = ps[:, 0:T_BLK] if bwd else ps[:, HALO:HALO + T_BLK]
                if bwd:
                    dst = dst[:, ::-1]
                s = nc.scalar.activation(dst, zsl, AF.Silu)
            pool_.release(bank, "cv_s", p.inc(s, "cv_s", 1))
        xn_rhs = lambda kc, a, b: xn[0:128, kc * T_EXT + w0 + a: kc * T_EXT + w0 + b]
        gemm(p, f"win{u}", lhsT_dram=W["w_inT"], K=D, M=4096, T=TW, rhs=xn_rhs,
             epilogue=epWIN, pool=pool, ws=ws, t_tile=512,
             rhs_ready=("ln_v", p.val("ln_v")))
        cvs = p.val("cv_s")
        nc.vector.wait_ge(p.sem("cv_s"), cvs)
        if blk == 0 and hmask is not None:
            dcol = 1 if bwd else 0
            for ch in range(16):
                v = nc.vector.tensor_scalar_mul(u_blk[:, ch * TW: ch * TW + HALO],
                                                u_blk[:, ch * TW: ch * TW + HALO],
                                                hmask[:, dcol:dcol + 1])
                p.inc(v, "sc_v", 1)
        for ch in range(16):
            ub = u_blk[:, ch * TW: (ch + 1) * TW]
            p.wait(nc.vector, "cv_s")
            v = nc.vector.tensor_scalar_mul(sp1[:, :], ub[:, 0:T_BLK], Wt["convw"][:, ch:ch + 1])
            p.inc(v, "sc_v", 1)
            for k in range(1, 4):
                v = nc.vector.scalar_tensor_tensor(
                    sp1[:, :], ub[:, k:k + T_BLK], Wt["convw"][:, 16 * k + ch: 16 * k + ch + 1],
                    sp1[:, :], ALU.mult, ALU.add)
                p.inc(v, "sc_v", 1)
            vv = p.val("sc_v")
            nc.scalar.wait_ge(p.sem("sc_v"), vv)
            s = nc.scalar.activation(ucsil[:, ch * T_BLK:(ch + 1) * T_BLK], sp1[:, :],
                                     AF.Silu, bias=Wt["convb"][:, ch:ch + 1])
            p.inc(s, "cv_s", 1)
            s = nc.scalar.copy(ucb[:, ch * T_BLK:(ch + 1) * T_BLK],
                               ucsil[:, ch * T_BLK:(ch + 1) * T_BLK])
            p.inc(s, "cv_s", 1)
        cvs = p.val("cv_s")
        if dbg_uc is not None and blk == 0:
            p.wait(nc.sync, "cv_s")
            for ch in range(16):
                p.inc(nc.sync.dma_start(out=dbg_uc[1][ch * 128:(ch + 1) * 128, :],
                                        in_=u_blk[:, ch * TW:(ch + 1) * TW]), "dpk", 16)
            p.wait(nc.scalar, "dpk")
        if dbg_uc is not None:
            p.wait(nc.sync, "cv_s")
            for ch in range(16):
                p.inc(nc.sync.dma_start(out=dbg_uc[0][ch * 128:(ch + 1) * 128,
                                                   blk * T_BLK:(blk + 1) * T_BLK],
                                        in_=ucsil[:, ch * T_BLK:(ch + 1) * T_BLK]), "dpk", 16)
            p.wait(nc.scalar, "dpk")
        def epXP(mo, t0, t1, ps, pool_, bank):
            p.wait(nc.scalar, "mm")
            p.wait(nc.scalar, "dpk")
            s = nc.scalar.copy(dtb_t[:, :], ps[0:64, :])
            p.inc(s, "xp_s", 1)
            s = nc.scalar.copy(bcrows[:, :], ps[64:96, :])
            pool_.release(bank, "xp_s", p.inc(s, "xp_s", 1))
        ucb_rhs = lambda kc, a, b: ucb[0:128, kc * T_BLK + a: kc * T_BLK + b]
        gemm(p, f"xp{u}", lhsT_dram=W["x_projT"], K=D_INNER, M=96, T=T_BLK,
             rhs=ucb_rhs, epilogue=epXP, pool=pool, ws=ws, t_tile=512,
             rhs_ready=("cv_s", cvs))
        def epDT(mo, t0, t1, ps, pool_, bank):
            dch = delta[:, mo * T_BLK:(mo + 1) * T_BLK]
            p.wait(nc.vector, "mm")
            p.wait(nc.vector, "dpk")
            p.wait(nc.scalar, "dpk")
            p.wait(nc.vector, "dt_s")
            v = nc.vector.tensor_scalar_add(sp1[:, :], ps, Wt["dtb"][:, mo:mo + 1])
            vv = p.inc(v, "sc_v", 1)
            pool_.release(bank, "sc_v", vv)
            v = nc.vector.tensor_scalar_max(sp2[:, :], sp1[:, :], 0.0)
            p.inc(v, "sc_v", 1)
            nc.scalar.wait_ge(p.sem("sc_v"), p.val("sc_v"))
            s = nc.scalar.activation(dch, sp1[:, :], AF.Abs)
            p.inc(s, "dt_s", 1)
            s = nc.scalar.activation(dch, dch, AF.Exp, scale=-1.0)
            p.inc(s, "dt_s", 1)
            s = nc.scalar.activation(dch, dch, AF.Ln, bias=1.0)
            sv = p.inc(s, "dt_s", 1)
            nc.vector.wait_ge(p.sem("dt_s"), sv)
            v = nc.vector.tensor_add(dch, dch, sp2[:, :])
            vv = p.inc(v, "sc_v", 1)
            nc.sync.wait_ge(p.sem("sc_v"), vv)
            p.inc(nc.sync.dma_start(out=delta_out[mo * 128:(mo + 1) * 128,
                                                  blk * T_BLK:(blk + 1) * T_BLK],
                                    in_=dch), "dpk", 16)
            v = nc.vector.reduce_sum(dred[:, :], dch, axis=AX.X)
            rdv = p.inc(v, "sc_v", 1)
            nc.vector.wait_ge(p.sem("sc_v"), rdv)
            v = nc.vector.tensor_add(dsum[:, mo:mo + 1], dsum[:, mo:mo + 1], dred[:, :])
            p.inc(v, "sc_v", 1)
        dtb_rhs = lambda kc, a, b: dtb_t[0:64, a:b]
        gemm(p, f"dt{u}", lhsT_dram=W["dt_wT"], K=RANK, M=D_INNER, T=T_BLK,
             rhs=dtb_rhs, epilogue=epDT, pool=pool, ws=ws, t_tile=512,
             rhs_ready=("xp_s", p.val("xp_s")))
        p.wait(nc.sync, "cv_s")
        for ch in range(16):
            p.inc(nc.sync.dma_start(out=zs_out[ch * 128:(ch + 1) * 128,
                                               blk * T_BLK:(blk + 1) * T_BLK],
                                    in_=zsil[:, ch * T_BLK:(ch + 1) * T_BLK]), "dpk", 16)
        p.wait(nc.sync, "xp_s")
        crow_src = bcrows[16:32, :]
        p.inc(nc.sync.dma_start(out=c_out[:, blk * T_BLK:(blk + 1) * T_BLK],
                                in_=crow_src), "dpk", 16)
        dltv = p.val("sc_v")
        for ch in range(16):
            dch = delta[:, ch * T_BLK:(ch + 1) * T_BLK]
            nc.gpsimd.wait_ge(p.sem("sc_v"), dltv)
            p.wait(nc.gpsimd, "sc_v")
            g = nc.gpsimd.tensor_tensor(du_t[:, :], dch,
                                        ucsil[:, ch * T_BLK:(ch + 1) * T_BLK], ALU.mult)
            duv = p.inc(g, "sc_g", 1)
            for n in range(NSTATE):
                col = ch * NSTATE + n
                p.wait(nc.gpsimd, "sc_s")
                g = nc.gpsimd.tensor_scalar_mul(aarg[:, :], dch, Wt["A"][:, col:col + 1])
                gv = p.inc(g, "sc_g", 1)
                nc.scalar.wait_ge(p.sem("sc_g"), gv)
                p.wait(nc.scalar, "sc_v")
                s = nc.scalar.activation(at[:, :], aarg[:, :], AF.Exp)
                sv = p.inc(s, "sc_s", 1)
                psB, bankB = pool.acquire()
                mm = nc.tensor.matmul(psB[:, 0:T_BLK], consts["ind32"][:, n * 128:(n + 1) * 128],
                                      bcrows[:, :], start=True, stop=True)
                mmv = p.inc(mm, "mm", 1)
                nc.vector.wait_ge(p.sem("mm"), mmv)
                nc.vector.wait_ge(p.sem("sc_g"), duv)
                v = nc.vector.tensor_tensor(bt[:, :], du_t[:, :], psB[:, 0:T_BLK], ALU.mult)
                bv = p.inc(v, "sc_v", 1)
                pool.release(bankB, "sc_v", bv)
                nc.vector.wait_ge(p.sem("sc_s"), sv)
                v = nc.vector.tensor_tensor_scan(ht[:, :], at[:, :], bt[:, :],
                                                 carry[:, col:col + 1], ALU.mult, ALU.add)
                hv = p.inc(v, "sc_v", 1)
                nc.vector.wait_ge(p.sem("sc_v"), hv)
                v = nc.vector.tensor_copy(carry[:, col:col + 1], ht[:, T_BLK - 1:T_BLK])
                cpv = p.inc(v, "sc_v", 1)
                nc.vector.wait_ge(p.sem("sc_v"), cpv)
                psC, bankC = pool.acquire()
                mm = nc.tensor.matmul(psC[:, 0:T_BLK],
                                      consts["ind32"][:, (16 + n) * 128:(17 + n) * 128],
                                      bcrows[:, :], start=True, stop=True)
                mmv = p.inc(mm, "mm", 1)
                nc.vector.wait_ge(p.sem("mm"), mmv)
                nc.vector.wait_ge(p.sem("b_rep"), p.val("b_rep"))
                v = nc.vector.tensor_tensor(ct[:, :], ht[:, :], psC[:, 0:T_BLK], ALU.mult)
                cv = p.inc(v, "sc_v", 1)
                pool.release(bankC, "sc_v", cv)
                nc.gpsimd.wait_ge(p.sem("sc_v"), cv)
                if n == 0:
                    g = nc.gpsimd.tensor_copy(ACC[:, ch * T_BLK:(ch + 1) * T_BLK], ct[:, :])
                else:
                    g = nc.gpsimd.tensor_tensor(ACC[:, ch * T_BLK:(ch + 1) * T_BLK],
                                                ACC[:, ch * T_BLK:(ch + 1) * T_BLK],
                                                ct[:, :], ALU.add)
                p.inc(g, "b_rep", 1)
        gv = p.val("b_rep")
        nc.vector.wait_ge(p.sem("b_rep"), gv)
        for ch in range(16):
            v = nc.vector.scalar_tensor_tensor(sp1[:, :], ucsil[:, ch * T_BLK:(ch + 1) * T_BLK],
                                               Wt["Dv"][:, ch:ch + 1],
                                               ACC[:, ch * T_BLK:(ch + 1) * T_BLK],
                                               ALU.mult, ALU.add)
            p.inc(v, "sc_v", 1)
            v = nc.vector.tensor_tensor(sp2[:, :], sp1[:, :],
                                        zsil[:, ch * T_BLK:(ch + 1) * T_BLK], ALU.mult)
            vv = p.inc(v, "sc_v", 1)
            nc.sync.wait_ge(p.sem("sc_v"), vv)
            p.inc(nc.sync.dma_start(out=p1_out[ch * 128:(ch + 1) * 128,
                                               blk * T_BLK:(blk + 1) * T_BLK],
                                    in_=sp2[:, :]), "dpk", 16)
            p.wait(nc.vector, "dpk")
    vv = p.val("sc_v")
    nc.gpsimd.wait_ge(p.sem("sc_v"), vv)
    p.wait(nc.gpsimd, "dpk")
    for ch in range(16):
        g = nc.gpsimd.tensor_scalar_mul(smout[:, 256 + ch * NSTATE: 256 + (ch + 1) * NSTATE],
                                        Wt["A"][:, ch * NSTATE:(ch + 1) * NSTATE],
                                        dsum[:, ch:ch + 1])
        p.inc(g, "sc_g", 1)
    gv = p.val("sc_g")
    nc.scalar.wait_ge(p.sem("sc_g"), gv)
    p.wait(nc.scalar, "dpk")
    s = nc.scalar.activation(smout[:, 256:512], smout[:, 256:512], AF.Exp)
    p.inc(s, "sm_s", 1)
    s = nc.scalar.copy(smout[:, 0:256], carry[:, :])
    sv = p.inc(s, "sm_s", 1)
    nc.sync.wait_ge(p.sem("sm_s"), sv)
    p.inc(nc.sync.dma_start(out=sm_out[:, :], in_=smout[:, :]), "dpk", 16)


def stage_ind32(p):
    ind = np.zeros((32, 32 * 128), np.float32)
    for n in range(32):
        ind[n, n * 128:(n + 1) * 128] = 1.0
    dr = p.din("c_ind32", ind)
    t = p.sbuf("c_ind32_t", [32, 32 * 128], F32)
    p.inc(p.nc.sync.dma_start(out=t[:, :], in_=dr[:, :]), "cld", 16)
    return t


def stage_ind16(p):
    ind = np.zeros((16, 16 * 128), np.float32)
    for n in range(16):
        ind[n, n * 128:(n + 1) * 128] = 1.0
    dr = p.din("c_ind16", ind)
    t = p.sbuf("c_ind16_t", [16, 16 * 128], F32)
    p.inc(p.nc.sync.dma_start(out=t[:, :], in_=dr[:, :]), "cld", 16)
    return t


def emit_double_ln(p, *, x_dram, xn_tile, xr_out, lng, lnb, consts, pool, scr,
                   xbuf=None, qt=None):
    nc = p.nc
    u = p.uid()
    if xbuf is None:
        xbuf = p.sbuf(f"dlnx{u}", [128, NCH * 512], F32)
    if qt is None:
        qt = p.sbuf(f"dlnq{u}", [128, NCH * 512], F32)
    xo = [p.sbuf(f"dlnxo{u}_{i}", [128, 512], F32) for i in range(2)]
    widths = [512, 512, 512, 390, 128]
    assert sum(widths) == T_EXT
    offs = [sum(widths[:i]) for i in range(len(widths))]
    for tt in range(len(widths)):
        t0, w = offs[tt], widths[tt]
        t1 = t0 + w
        p.wait(nc.sync, "ln_v")
        p.wait(nc.sync, "ln_s")
        p.wait(nc.sync, "mm")
        p.wait(nc.sync, "xr_v")
        for j in range(NCH):
            p.inc(nc.sync.dma_start(out=xbuf[:, j * 512: j * 512 + w],
                                    in_=x_dram[j * 128:(j + 1) * 128, t0:t1]), "xld", 16)
        p.wait(nc.scalar, "xld")
        p.wait(nc.tensor, "xld")
        p.wait(nc.vector, "xld")
        layernorm(p, f"dln1_{u}_{tt}", X=lambda j, a, b: xbuf[:, j * 512 + a: j * 512 + b],
                  T=w, scr=scr, out_f32=lambda j, a, b: qt[:, j * 512 + a: j * 512 + b],
                  pool=pool, ones_f32=consts["ones_f32"], eps_ap=consts["eps"][0:1, 0:1])
        c0, c1 = max(t0, HALO), min(t1, HALO + TC)
        if c1 > c0:
            p.wait(nc.vector, "ln_v")
            for j in range(NCH):
                par = j % 2
                p.wait(nc.vector, "xrst", max(0, p.val("xrst") - 16))
                v = nc.vector.tensor_add(xo[par][:, 0:c1 - c0],
                                         xbuf[:, j * 512 + c0 - t0: j * 512 + c1 - t0],
                                         qt[:, j * 512 + c0 - t0: j * 512 + c1 - t0])
                xv = p.inc(v, "xr_v", 1)
                nc.sync.wait_ge(p.sem("xr_v"), xv)
                p.inc(nc.sync.dma_start(out=xr_out[j * 128:(j + 1) * 128, c0 - HALO:c1 - HALO],
                                        in_=xo[par][:, 0:c1 - c0]), "xrst", 16)
        p.wait(nc.scalar, "ln_v")
        p.wait(nc.tensor, "ln_v")
        layernorm(p, f"dln2_{u}_{tt}", X=lambda j, a, b: qt[:, j * 512 + a: j * 512 + b],
                  T=w, scr=scr,
                  out_bf=lambda j, a, b: xn_tile[:, j * T_EXT + t0 + a: j * T_EXT + t0 + b],
                  gain=lng, gbias=lnb,
                  pool=pool, ones_f32=consts["ones_f32"], eps_ap=consts["eps"][0:1, 0:1])


def make_A_tile(p, W, tag):
    nc = p.nc
    t = p.sbuf(f"A_{tag}", [128, 16 * NSTATE], F32)
    p.inc(nc.sync.dma_start(out=t[:, :], in_=W["alog_d"][:, :]), "cld2", 16)
    p.wait(nc.scalar, "cld2")
    s = nc.scalar.activation(t[:, :], t[:, :], AF.Exp)
    sv = p.inc(s, "a_s", 1)
    nc.vector.wait_ge(p.sem("a_s"), sv)
    v = nc.vector.tensor_scalar_mul(t[:, :], t[:, :], -1.0)
    p.inc(v, "a_v", 1)
    return t


def run_mamba_a(x, d):
    p = Prog()
    nc = p.nc
    x_dram = p.din("x", shard_x_halo(x))
    hm = []
    for c in range(NCORE):
        q = c % 4
        m = np.ones((128, 2), np.float32)
        if q == 0:
            m[:, 0] = 0.0
        if q == 3:
            m[:, 1] = 0.0
        hm.append(m)
    hm_t = load_small(p, "hm_t", p.din("hmask", hm), [128, 2])
    xr_out = p.dout("xr", [D, TC])
    outs = {}
    for tag in ("f", "b"):
        outs[f"delta_{tag}"] = p.dout(f"delta_{tag}", [D_INNER, TC])
        outs[f"p1_{tag}"] = p.dout(f"p1_{tag}", [D_INNER, TC])
        outs[f"zs_{tag}"] = p.dout(f"zs_{tag}", [D_INNER, TC])
        outs[f"crows_{tag}"] = p.dout(f"crows_{tag}", [NSTATE, TC])
        outs[f"summ_{tag}"] = p.dout(f"summ_{tag}", [128, 2 * 16 * NSTATE])
    consts = stage_consts(p)
    consts["ind32"] = stage_ind32(p)
    lng = load_small(p, "lng_t", p.din("lng", chunk_vec(np.asarray(d["L0_attn_a_ln_g"]))), [128, NCH])
    lnb = load_small(p, "lnb_t", p.din("lnb", chunk_vec(np.asarray(d["L0_attn_a_ln_b"]))), [128, NCH])
    Wd, Wt = {}, {}
    for tag in ("f", "b"):
        Wd[tag] = stage_mamba_consts(p, d, tag, nc)
        Wt[tag] = {
            "convw": load_small(p, f"convw_t{tag}", Wd[tag]["convw_d"], [128, 64]),
            "convb": load_small(p, f"convb_t{tag}", Wd[tag]["convb_d"], [128, 16]),
            "dtb": load_small(p, f"dtb_t{tag}", Wd[tag]["dtb_d"], [128, 16]),
            "Dv": load_small(p, f"Dv_t{tag}", Wd[tag]["D_d"], [128, 16]),
            "A": make_A_tile(p, Wd[tag], tag),
        }
    pool = PsumPool(p, "pp", banks=5)
    scr = make_ln_scratch(p)
    ws = WStream(p, "ws", 4096)
    p.join_engines(sems=["cld", "a_v"])
    xn = p.sbuf("xn", [128, NCH * T_EXT], BF16)
    bufs = make_mamba_bufs(p)
    emit_double_ln(p, x_dram=x_dram, xn_tile=xn, xr_out=xr_out, lng=lng, lnb=lnb,
                   consts=consts, pool=pool, scr=scr,
                   xbuf=bufs["delta"], qt=bufs["ACC"])
    p.join_engines()
    for tag in ("f", "b"):
        emit_mamba_dir(p, Wd[tag], Wt[tag], xn=xn, consts=consts, pool=pool,
                       dirtag=tag, delta_out=outs[f"delta_{tag}"],
                       p1_out=outs[f"p1_{tag}"], zs_out=outs[f"zs_{tag}"],
                       c_out=outs[f"crows_{tag}"], sm_out=outs[f"summ_{tag}"], ws=ws,
                       bufs=bufs, hmask=hm_t)
    p.wait(nc.sync, "dpk")
    p.wait(nc.sync, "xrst")
    res = p.run()
    p.close()
    keys = ["xr"] + [f"{k}_{t}" for t in ("f", "b")
                     for k in ("delta", "p1", "zs", "crows", "summ")]
    return {k: [r[k] for r in res.results] for k in keys}


def run_mamba_b(ph1, d):
    p = Prog()
    nc = p.nc
    xr_dram = p.din("xr", ph1["xr"])
    din = {}
    for tag in ("f", "b"):
        for k in ("delta", "p1", "zs"):
            din[f"{k}_{tag}"] = p.din(f"{k}_{tag}", ph1[f"{k}_{tag}"])
        din[f"crows_{tag}"] = p.din(f"crows_{tag}", ph1[f"crows_{tag}"])
        summs = []
        for c in range(NCORE):
            b, q = c // 4, c % 4
            lst = []
            for i in range(1, 4):
                qq = q - i if tag == "f" else q + i
                if 0 <= qq < 4:
                    lst.append(np.asarray(ph1[f"summ_{tag}"][b * 4 + qq]))
                else:
                    lst.append(np.zeros((128, 512), np.float32))
            summs.append(np.stack(lst))
        din[f"summ_{tag}"] = p.din(f"summ3_{tag}", summs)
    w_outT = {t: p.din(f"w_outT_{t}", f32c(np.asarray(d[f"L0_attn_a_w_out_{t}"]).T))
              for t in ("f", "b")}
    projT = p.din("projT", f32c(np.asarray(d["L0_attn_a_proj_w"]).T))
    projb = load_small(p, "projb_t", p.din("projb", chunk_vec(np.asarray(d["L0_attn_a_proj_b"]))),
                       [128, NCH])
    consts = stage_consts(p)
    consts["ind16"] = stage_ind16(p)
    alog = {}
    for tag in ("f", "b"):
        ac = np.zeros((128, 16 * NSTATE), np.float32)
        al = np.asarray(d[f"L0_attn_a_A_log_{tag}"])
        for ch in range(16):
            ac[:, ch * NSTATE:(ch + 1) * NSTATE] = al[ch * 128:(ch + 1) * 128, :]
        alog[tag] = {"alog_d": p.din(f"alog2_{tag}", f32c(ac))}
    A = {tag: make_A_tile(p, alog[tag], tag) for tag in ("f", "b")}
    y_dram = p.dout("y", [D, TC])
    pool = PsumPool(p, "pp", banks=6)
    ws = WStream(p, "ws", 4096)
    FB = p.sbuf("FB", [128, 16 * TC], BF16)
    zeros_t = p.sbuf("zeros_t", [128, T_BLK], F32)
    p.inc(nc.vector.memset(zeros_t[:, :], 0.0), "sc_v", 1)
    p.join_engines(sems=["cld", "cld2", "a_v"])
    u = p.uid()
    dblk = p.sbuf(f"dblk{u}", [128, 16 * T_BLK], F32)
    zblk = p.sbuf(f"zblk{u}", [128, 16 * T_BLK], F32)
    pblk = p.sbuf(f"pblk{u}", [128, 16 * T_BLK], F32)
    cblk = p.sbuf(f"cblk{u}", [16, T_BLK], F32)
    yb = p.sbuf(f"yb{u}", [128, 16 * T_BLK], BF16)
    aarg = p.sbuf(f"aarg{u}", [128, T_BLK], F32)
    at = p.sbuf(f"at{u}", [128, T_BLK], F32)
    gt = p.sbuf(f"gt{u}", [128, T_BLK], F32)
    ct = p.sbuf(f"ct{u}", [128, T_BLK], F32)
    ACC = p.sbuf(f"acc2{u}", [128, 16 * T_BLK], F32)
    sm = p.sbuf(f"sm{u}", [128, 3 * 512], F32)
    hin = p.sbuf(f"hin{u}", [128, 256], F32)
    t1_ = p.sbuf(f"t1_{u}", [128, 256], F32)
    carry = p.sbuf(f"carry2{u}", [128, 256], F32)
    for tag in ("f", "b"):
        bwd = tag == "b"
        for i in range(3):
            p.inc(nc.sync.dma_start(out=sm[:, i * 512:(i + 1) * 512],
                                    in_=din[f"summ_{tag}"][i, :, :]), "sld", 16)
        p.wait(nc.vector, "sld")
        p.wait(nc.vector, "sc_s")
        p.wait(nc.vector, "mm")
        v = nc.vector.tensor_tensor(t1_[:, :], sm[:, 512 + 256:1024], sm[:, 1024:1024 + 256], ALU.mult)
        p.inc(v, "sc_v", 1)
        v = nc.vector.tensor_add(t1_[:, :], sm[:, 512:512 + 256], t1_[:, :])
        p.inc(v, "sc_v", 1)
        v = nc.vector.tensor_tensor(t1_[:, :], sm[:, 256:512], t1_[:, :], ALU.mult)
        p.inc(v, "sc_v", 1)
        v = nc.vector.tensor_add(hin[:, :], sm[:, 0:256], t1_[:, :])
        p.inc(v, "sc_v", 1)
        v = nc.vector.tensor_copy(carry[:, :], hin[:, :])
        p.inc(v, "sc_v", 1)
        for blk in range(NBLK):
            p.wait(nc.sync, "sc_g")
            p.wait(nc.sync, "sc_v")
            p.wait(nc.sync, "y2_v")
            for nm, dst in (("delta", dblk), ("zs", zblk), ("p1", pblk)):
                p.inc(nc.sync.dma_start(
                    out=dst[:, :].rearrange("p (c t) -> p c t", t=T_BLK),
                    in_=din[f"{nm}_{tag}"][:, blk * T_BLK:(blk + 1) * T_BLK].rearrange(
                        "(c p) t -> p c t", p=128)), "dld", 16)
            p.inc(nc.sync.dma_start(out=cblk[:, :],
                                    in_=din[f"crows_{tag}"][:, blk * T_BLK:(blk + 1) * T_BLK]),
                  "dld", 16)
            dldv = p.val("dld")
            nc.gpsimd.wait_ge(p.sem("dld"), dldv)
            nc.tensor.wait_ge(p.sem("dld"), dldv)
            for ch in range(16):
                dch = dblk[:, ch * T_BLK:(ch + 1) * T_BLK]
                for n in range(NSTATE):
                    col = ch * NSTATE + n
                    p.wait(nc.gpsimd, "sc_s")
                    g = nc.gpsimd.tensor_scalar_mul(aarg[:, :], dch, A[tag][:, col:col + 1])
                    gv = p.inc(g, "sc_g", 1)
                    nc.scalar.wait_ge(p.sem("sc_g"), gv)
                    p.wait(nc.scalar, "sc_v")
                    s = nc.scalar.activation(at[:, :], aarg[:, :], AF.Exp)
                    sv = p.inc(s, "sc_s", 1)
                    nc.vector.wait_ge(p.sem("sc_s"), sv)
                    v = nc.vector.tensor_tensor_scan(gt[:, :], at[:, :], zeros_t[:, :],
                                                     carry[:, col:col + 1], ALU.mult, ALU.add)
                    hv = p.inc(v, "sc_v", 1)
                    nc.vector.wait_ge(p.sem("sc_v"), hv)
                    v = nc.vector.tensor_copy(carry[:, col:col + 1], gt[:, T_BLK - 1:T_BLK])
                    cpv = p.inc(v, "sc_v", 1)
                    nc.vector.wait_ge(p.sem("sc_v"), cpv)
                    psC, bankC = pool.acquire()
                    mm = nc.tensor.matmul(psC[:, 0:T_BLK],
                                          consts["ind16"][:, n * 128:(n + 1) * 128],
                                          cblk[:, :], start=True, stop=True)
                    mmv = p.inc(mm, "mm", 1)
                    nc.vector.wait_ge(p.sem("mm"), mmv)
                    nc.vector.wait_ge(p.sem("b_rep"), p.val("b_rep"))
                    v = nc.vector.tensor_tensor(ct[:, :], gt[:, :], psC[:, 0:T_BLK], ALU.mult)
                    cv = p.inc(v, "sc_v", 1)
                    pool.release(bankC, "sc_v", cv)
                    nc.gpsimd.wait_ge(p.sem("sc_v"), cv)
                    if n == 0:
                        g = nc.gpsimd.tensor_copy(ACC[:, ch * T_BLK:(ch + 1) * T_BLK], ct[:, :])
                    else:
                        g = nc.gpsimd.tensor_tensor(ACC[:, ch * T_BLK:(ch + 1) * T_BLK],
                                                    ACC[:, ch * T_BLK:(ch + 1) * T_BLK],
                                                    ct[:, :], ALU.add)
                    p.inc(g, "b_rep", 1)
            gv = p.val("b_rep")
            nc.vector.wait_ge(p.sem("b_rep"), gv)
            p.wait(nc.vector, "mm")
            for ch in range(16):
                v = nc.vector.tensor_tensor(ct[:, :], ACC[:, ch * T_BLK:(ch + 1) * T_BLK],
                                            zblk[:, ch * T_BLK:(ch + 1) * T_BLK], ALU.mult)
                p.inc(v, "sc_v", 1)
                v = nc.vector.tensor_add(yb[:, ch * T_BLK:(ch + 1) * T_BLK],
                                         ct[:, :], pblk[:, ch * T_BLK:(ch + 1) * T_BLK])
                p.inc(v, "y2_v", 1)
            if bwd:
                g0 = TC - (blk + 1) * T_BLK
            else:
                g0 = blk * T_BLK
            def epWO(mo, t0, t1, ps, pool_, bank, g0=g0, bwd=bwd, tag=tag):
                p.wait(nc.scalar, "mm")
                kcidx = (8 if tag == "b" else 0) + mo
                dst = FB[:, kcidx * TC + g0: kcidx * TC + g0 + T_BLK]
                if bwd:
                    dst = dst[:, ::-1]
                s = nc.scalar.copy(dst, ps)
                pool_.release(bank, "fb_s", p.inc(s, "fb_s", 1))
            yb_rhs = lambda kc, a, b: yb[0:128, kc * T_BLK + a: kc * T_BLK + b]
            gemm(p, f"wo{tag}{blk}", lhsT_dram=w_outT[tag], K=D_INNER, M=D, T=T_BLK,
                 rhs=yb_rhs, epilogue=epWO, pool=pool, ws=ws, t_tile=512,
                 rhs_ready=("y2_v", p.val("y2_v")))
    xi = [p.sbuf(f"xi{i}", [128, 512], F32) for i in range(2)]
    yo = [p.sbuf(f"yo{i}", [128, 512], F32) for i in range(2)]
    def epPJ(mo, t0, t1, ps, pool_, bank):
        it = mo * 4 + t0 // 512
        par = it % 2
        if it >= 2:
            p.wait(nc.sync, "y2_v")
        p.inc(nc.sync.dma_start(out=xi[par][:, :],
                                in_=xr_dram[mo * 128:(mo + 1) * 128, t0:t1]), "xld", 16)
        xldv = p.val("xld")
        nc.vector.wait_ge(p.sem("xld"), xldv)
        p.wait(nc.vector, "mm")
        if it >= 2:
            p.wait(nc.vector, "yst")
        v = nc.vector.scalar_tensor_tensor(yo[par][:, :], ps, projb[:, mo:mo + 1],
                                           xi[par][:, :], ALU.add, ALU.add)
        yv = p.inc(v, "y2_v", 1)
        pool_.release(bank, "y2_v", yv)
        nc.sync.wait_ge(p.sem("y2_v"), yv)
        p.inc(nc.sync.dma_start(out=y_dram[mo * 128:(mo + 1) * 128, t0:t1],
                                in_=yo[par][:, :]), "yst", 16)
    FB_rhs = lambda kc, a, b: FB[0:128, kc * TC + a: kc * TC + b]
    gemm(p, "proj", lhsT_dram=projT, K=D_INNER, M=D, T=TC, rhs=FB_rhs, epilogue=epPJ,
         pool=pool, ws=ws, rhs_ready=("fb_s", p.val("fb_s")))
    p.wait(nc.sync, "yst")
    res = p.run()
    p.close()
    return [r["y"] for r in res.results]

WINDOW = 256
HB = 258
TK = TC + WINDOW


def _l0ffn_tables():
    DIM_HEAD, n = 64, SEQ
    half = np.arange(0, DIM_HEAD, 2, dtype=np.float64)
    inv_freq = 1.0 / 10000.0 ** (half / DIM_HEAD)
    t = np.arange(n, dtype=np.float64)
    freqs = t[:, None] * inv_freq[None, :]
    freqs = np.concatenate([freqs, freqs], -1)
    base_scale = (half + 0.4 * DIM_HEAD) / (1.4 * DIM_HEAD)
    power = (t - n // 2) / (WINDOW // 2)
    scale = base_scale[None, :] ** power[:, None]
    scale = np.concatenate([scale, scale], -1)
    cq = np.cos(freqs) * scale
    sq = np.sin(freqs) * scale
    ckk = np.cos(freqs) / scale
    sk = np.sin(freqs) / scale
    return [np.tile(a.T, (2, 1)).astype(np.float32) for a in (cq, sq, ckk, sk)]


_L0_TABLES = None


def l0ffn_core_tables(c):
    global _L0_TABLES
    if _L0_TABLES is None:
        _L0_TABLES = _l0ffn_tables()
    q = c % 4
    g0 = q * TC - HB
    out = []
    for a in _L0_TABLES:
        buf = np.zeros((128, HB + TC), np.float32)
        s0 = max(0, g0)
        buf[:, s0 - g0: TC + HB] = a[:, s0: g0 + HB + TC]
        out.append(f32c(buf))
    return out


def l0ffn_masks(c):
    q = c % 4
    m = np.zeros((4, 128, 2 * WINDOW), np.float32)
    P = np.arange(2 * WINDOW)[None, :]
    for qt2 in range(2):
        qp = (qt2 * 128 + np.arange(128))[:, None]
        allowed = (P >= qp) & (P <= qp + WINDOW)
        m[qt2] = np.where(allowed, 0.0, -1e30)
        aw0 = allowed & (P >= WINDOW) if q == 0 else allowed
        m[2 + qt2] = np.where(aw0, 0.0, -1e30)
    return f32c(m)


def _rope_R():
    R = np.zeros((128, 128), np.float32)
    for b in range(2):
        for dd in range(32):
            R[b * 64 + dd, b * 64 + 32 + dd] = -1.0
            R[b * 64 + 32 + dd, b * 64 + dd] = 1.0
    return f32c(R.T)


def run_l0ffn_ln(x_parts, d):
    p = Prog()
    nc = p.nc
    x_dram = p.din("x", x_parts)
    y_dram = p.dout("y", [D, TC])
    consts = stage_consts(p)
    pool = PsumPool(p, "pp", banks=6)
    scr = make_ln_scratch(p)
    p.join_engines(sems=["cld"])
    ln = p.sbuf("lnx", [128, NCH * TC], F32)
    emit_ln_from_dram(p, "lnl0", x_dram=x_dram, xn_tile=None, xf_tile=ln, T=TC,
                      consts=consts, pool=pool, scr=scr)
    p.wait(nc.sync, "ln_v")
    for j in range(NCH):
        p.inc(nc.sync.dma_start(out=y_dram[j * 128:(j + 1) * 128, :],
                                in_=ln[:, j * TC:(j + 1) * TC]), "yst", 16)
    p.wait(nc.sync, "yst")
    res = p.run()
    p.close()
    return [r["y"] for r in res.results]


def _shifted_col(kc, col):
    return col - (1 if kc >= 4 else 0)


def stage_ind16_bf(p):
    ind = np.zeros((16, 16 * 128), np.float32)
    for n in range(16):
        ind[n, n * 128:(n + 1) * 128] = 1.0
    dr = p.din("c_ind16b", ind)
    t = p.sbuf("c_ind16b_t", [16, 16 * 128], BF16)
    p.inc(p.nc.gpsimd.dma_start(out=t[:, :], in_=dr[:, :]), "cld", 16)
    return t


def run_l0ffn_attn(xl_parts, d, layer):
    p = Prog()
    nc = p.nc
    xl_dram = p.din("xl", xl_parts)
    y_dram = p.dout("y", [D, TC])
    Wqkv = np.asarray(d["L0_ffn_Wqkv"][layer])
    Wq_d = p.din("Wq", f32c(Wqkv[:, 0:D]))
    Wk_d = p.din("Wk", f32c(Wqkv[:, D:2 * D]))
    Wv_d = p.din("Wv", f32c(Wqkv[:, 2 * D:3 * D]))
    Wo_d = p.din("Wo", f32c(np.asarray(d["L0_ffn_Wo"][layer])))
    qs_d = p.din("qs", f32c(np.tile(np.asarray(d["L0_ffn_q_scale"][layer]), 2).reshape(128, 1)))
    ks_d = p.din("ks", f32c(np.tile(np.asarray(d["L0_ffn_k_scale"][layer]), 2).reshape(128, 1)))
    tabs = [l0ffn_core_tables(c) for c in range(NCORE)]
    tab_d = [p.din(f"tab{i}", [tabs[c][i] for c in range(NCORE)]) for i in range(4)]
    mask_d = p.din("masks", [l0ffn_masks(c) for c in range(NCORE)])
    R_d = p.din("Rrope", _rope_R())
    consts = stage_consts(p)
    consts["ind16"] = stage_ind16_bf(p)
    ic = stage_ind_consts(p)
    pool = PsumPool(p, "pp", banks=6)
    ws = WStream(p, "ws", 4096)
    u = p.uid()
    XSB = p.sbuf(f"XSB{u}", [128, NCH * (HB + TC)], BF16)
    KP = p.sbuf(f"KP{u}", [128, NCH * TK], BF16)
    VT = p.sbuf(f"VT{u}", [128, (TK // 128) * 1024], BF16)
    WKr = p.sbuf(f"WKr{u}", [128, NCH * D], BF16)
    WVr = WKr
    WQr = WKr
    Rr = p.sbuf(f"Rr{u}", [128, 128], BF16)
    qsT = p.sbuf(f"qsT{u}", [128, 1], F32)
    ksT = p.sbuf(f"ksT{u}", [128, 1], F32)
    tabs_t = {2: p.sbuf(f"tab2{u}", [128, HB + TC], BF16),
              3: p.sbuf(f"tab3{u}", [128, HB + TC], BF16)}
    qtc = p.sbuf(f"qtc{u}", [128, WINDOW], BF16)
    qts = p.sbuf(f"qts{u}", [128, WINDOW], BF16)
    masks_t = p.sbuf(f"mask{u}", [128, 4 * 2 * WINDOW], BF16)
    ident_t = p.sbuf(f"ident{u}", [128, 128], BF16)
    sq1 = p.sbuf(f"sq1{u}", [128, 512], F32)
    nrow = p.sbuf(f"nrow{u}", [16, 512], F32)
    nrow2 = p.sbuf(f"nrow2{u}", [16, 512], F32)
    kraw = p.sbuf(f"kraw{u}", [128, 512], F32)
    kn1 = p.sbuf(f"kn1{u}", [128, 512], F32)
    kp1 = p.sbuf(f"kp1{u}", [128, 512], F32)
    QP = p.sbuf(f"QP{u}", [128, NCH * WINDOW], BF16)
    ES = p.sbuf(f"ES{u}", [128, 2 * WINDOW], BF16)
    AO = p.sbuf(f"AO{u}", [128, NCH * WINDOW], BF16)
    vband = p.sbuf(f"vband{u}", [128, 1024], BF16)
    xres = [p.sbuf(f"xres{u}_{i}", [128, WINDOW], F32) for i in range(2)]
    yo = [p.sbuf(f"yo{u}_{i}", [128, WINDOW], F32) for i in range(2)]
    for j in range(NCH):
        p.inc(nc.gpsimd.dma_start(out=XSB[:, j * (HB + TC):(j + 1) * (HB + TC)],
                                  in_=xl_dram[j * 128:(j + 1) * 128, :]), "cld", 16)
    for W_, dst in ((Wk_d, WKr),):
        src = W_[:, :].rearrange("(kc kp) m -> kp kc m", kp=128)
        dd = dst[:, :].rearrange("p (kc m) -> p kc m", m=D)
        p.inc(nc.gpsimd.dma_start(out=dd, in_=src), "cld", 16)
    p.inc(nc.gpsimd.dma_start(out=Rr[:, :], in_=R_d[:, :]), "cld", 16)
    p.inc(nc.sync.dma_start(out=qsT[:, :], in_=qs_d[:, :]), "cld", 16)
    p.inc(nc.sync.dma_start(out=ksT[:, :], in_=ks_d[:, :]), "cld", 16)
    for i in (2, 3):
        p.inc(nc.gpsimd.dma_start(out=tabs_t[i][:, :], in_=tab_d[i][:, :]), "cld", 16)
    p.inc(nc.gpsimd.dma_start(
        out=masks_t[:, :].rearrange("p (a w) -> p a w", w=2 * WINDOW),
        in_=mask_d[:, :, :].rearrange("a p w -> p a w")), "cld", 16)
    ident_d = p.din("identb", np.eye(128, dtype=np.float32))
    p.inc(nc.gpsimd.dma_start(out=ident_t[:, :], in_=ident_d[:, :]), "cld", 16)
    p.join_engines(sems=["cld"])

    def norm_rope(raw_ps, dst_bf, dst_norm, mo, w, tab_c, tab_s, sc_tile, tag):
        p.wait(nc.scalar, "mm")
        p.wait(nc.scalar, tag)
        s = nc.scalar.copy(kraw[:, 0:w], raw_ps)
        p.inc(s, "n_s", 1)
        s = nc.scalar.activation(sq1[:, 0:w], raw_ps, AF.Square)
        sv = p.inc(s, "n_s", 1)
        psN, bN = pool.acquire()
        nc.tensor.wait_ge(p.sem("n_s"), sv)
        mm = nc.tensor.matmul(psN[0:16, 0:w], ic["ind"][:, mo * 16:(mo + 1) * 16],
                              sq1[:, 0:w], start=True, stop=True)
        mmv = p.inc(mm, "mm", 1)
        nc.scalar.wait_ge(p.sem("mm"), mmv)
        p.wait(nc.scalar, "n_v")
        s = nc.scalar.activation(nrow[:, 0:w], psN[0:16, 0:w], AF.Sqrt)
        sv = p.inc(s, "n_s", 1)
        pool.release(bN, "n_s", sv)
        nc.vector.wait_ge(p.sem("n_s"), sv)
        v = nc.vector.tensor_scalar_max(nrow2[:, 0:w], nrow[:, 0:w], 1e-12)
        cv = p.inc(v, "n_v", 1)
        nc.scalar.wait_ge(p.sem("n_v"), cv)
        s = nc.scalar.activation(nrow[:, 0:w], nrow2[:, 0:w], AF.Ln)
        p.inc(s, "n_s", 1)
        s = nc.scalar.activation(nrow[:, 0:w], nrow[:, 0:w], AF.Exp, scale=-1.0)
        rv = p.inc(s, "n_s", 1)
        psR, bR = pool.acquire()
        nc.tensor.wait_ge(p.sem("n_s"), rv)
        mm = nc.tensor.matmul(psR[:, 0:w], ic["indT"][:, mo * 128:(mo + 1) * 128],
                              nrow[:, 0:w], start=True, stop=True)
        mmv = p.inc(mm, "mm", 1)
        nc.vector.wait_ge(p.sem("mm"), mmv)
        v = nc.vector.scalar_tensor_tensor(kn1[:, 0:w], kraw[:, 0:w], sc_tile[:, 0:1],
                                           psR[:, 0:w], ALU.mult, ALU.mult)
        knv = p.inc(v, "n_v", 1)
        pool.release(bR, "n_v", knv)
        knb = ES
        p.wait(nc.scalar, "n_v", knv)
        s = nc.scalar.copy(kp1[:, 0:w], kn1[:, 0:w])
        p.inc(s, "n_s", 1)
        s = nc.scalar.copy(dst_bf, kn1[:, 0:w])
        sv = p.inc(s, "n_s", 1)
        psT, bT = pool.acquire()
        nc.tensor.wait_ge(p.sem("n_s"), sv)
        mm = nc.tensor.matmul(psT[:, 0:w], Rr[:, :], dst_bf, start=True, stop=True)
        mmv = p.inc(mm, "mm", 1)
        nc.vector.wait_ge(p.sem("mm"), mmv)
        v = nc.vector.tensor_tensor(kn1[:, 0:w], kp1[:, 0:w], tab_c, ALU.mult)
        p.inc(v, "n_v", 1)
        v = nc.vector.tensor_tensor(kp1[:, 0:w], psT[:, 0:w], tab_s, ALU.mult)
        vv = p.inc(v, "n_v", 1)
        pool.release(bT, "n_v", vv)
        v = nc.vector.tensor_add(kn1[:, 0:w], kn1[:, 0:w], kp1[:, 0:w])
        vv = p.inc(v, "n_v", 1)
        nc.scalar.wait_ge(p.sem("n_v"), vv)
        s = nc.scalar.copy(dst_bf, kn1[:, 0:w])
        sv = p.inc(s, "n_s", 1)
        return sv

    def norm_rows(src_tile, TT_, dst_rows, sem_out):
        for tt in range((TT_ + 511) // 512):
            t0, t1 = tt * 512, min(TT_, (tt + 1) * 512)
            w = t1 - t0
            psN, bN = pool.acquire()
            for mo in range(NCH):
                p.wait(nc.scalar, "mm")
                s = nc.scalar.activation(sq1[:, 0:w],
                                         src_tile[:, mo * TT_ + t0: mo * TT_ + t1], AF.Square)
                sv = p.inc(s, "n_s", 1)
                nc.tensor.wait_ge(p.sem("n_s"), sv)
                mm = nc.tensor.matmul(psN[0:16, 0:w], ic["ind"][:, mo * 16:(mo + 1) * 16],
                                      sq1[:, 0:w], start=(mo == 0), stop=(mo == NCH - 1))
                p.inc(mm, "mm", 1)
            mmv = p.val("mm")
            nc.scalar.wait_ge(p.sem("mm"), mmv)
            s = nc.scalar.activation(dst_rows[:, t0:t1], psN[0:16, 0:w], AF.Sqrt)
            sv = p.inc(s, sem_out, 1)
            pool.release(bN, sem_out, sv)

    HT = HB + TC
    nc.tensor.wait_ge(p.sem("cld"), p.val("cld"))
    for mo in range(NCH):
        for tt in range((TK + 511) // 512):
            t0, t1 = tt * 512, min(TK, (tt + 1) * 512)
            w = t1 - t0
            psK, bK = pool.acquire()
            for kc in range(NCH):
                col = t0 + 2 - (1 if kc >= 4 else 0)
                mm = nc.tensor.matmul(psK[:, 0:w],
                                      WKr[:, kc * D + mo * 128: kc * D + (mo + 1) * 128],
                                      XSB[:, kc * HT + col: kc * HT + col + w],
                                      start=(kc == 0), stop=(kc == NCH - 1))
            p.inc(mm, "mm", 1)
            sv = norm_rope(psK[:, 0:w], KP[:, mo * TK + t0: mo * TK + t1],
                           None, mo, w,
                           tabs_t[2][:, t0 + 2: t0 + 2 + w], tabs_t[3][:, t0 + 2: t0 + 2 + w],
                           ksT, "n_v")
            pool.release(bK, "n_s", sv)
    p.wait(nc.gpsimd, "mm")
    p.inc(nc.gpsimd.dma_start(out=WVr[:, :].rearrange("p (kc m) -> p kc m", m=D),
                              in_=Wv_d[:, :].rearrange("(kc kp) m -> kp kc m", kp=128)),
          "cld2", 16)
    nc.tensor.wait_ge(p.sem("cld2"), p.val("cld2"))
    for tc in range(TK // 128):
        for half in range(2):
            psV, bV = pool.acquire()
            for kc in range(NCH):
                col = tc * 128 + 2 - (1 if kc >= 4 else 0)
                mm = nc.tensor.matmul(psV[:, :],
                                      XSB[:, kc * HT + col: kc * HT + col + 128],
                                      WVr[:, kc * D + half * 512: kc * D + (half + 1) * 512],
                                      start=(kc == 0), stop=(kc == NCH - 1))
            mmv = p.inc(mm, "mm", 1)
            nc.scalar.wait_ge(p.sem("mm"), mmv)
            s = nc.scalar.copy(VT[:, tc * 1024 + half * 512: tc * 1024 + (half + 1) * 512], psV)
            sv = p.inc(s, "n_s", 1)
            pool.release(bV, "n_s", sv)
    p.wait(nc.gpsimd, "mm")
    p.inc(nc.gpsimd.dma_start(out=WQr[:, :].rearrange("p (kc m) -> p kc m", m=D),
                              in_=Wq_d[:, :].rearrange("(kc kp) m -> kp kc m", kp=128)),
          "cld2", 16)
    nc.tensor.wait_ge(p.sem("cld2"), p.val("cld2"))
    negmax = p.sbuf(f"negmax{u}", [128, 1], F32)
    rsum = p.sbuf(f"rsum{u}", [128, 1], F32)
    EST = p.sbuf(f"est{u}", [128, 4 * 128], BF16)
    psT_t = p.psum(f"psTt{u}", [128, 128], BF16)
    for w_i in range(8):
        q0 = w_i * WINDOW
        p.wait(nc.gpsimd, "n_v")
        p.inc(nc.gpsimd.dma_start(out=qtc[:, :], in_=tab_d[0][:, q0 + HB: q0 + HB + WINDOW]),
              "qtl", 16)
        p.inc(nc.gpsimd.dma_start(out=qts[:, :], in_=tab_d[1][:, q0 + HB: q0 + HB + WINDOW]),
              "qtl", 16)
        qtlv = p.val("qtl")
        nc.vector.wait_ge(p.sem("qtl"), qtlv)
        for mo in range(NCH):
            psQ, bQ = pool.acquire()
            for kc in range(NCH):
                col = q0 + HB + 0 - (1 if kc >= 4 else 0)
                mm = nc.tensor.matmul(psQ[:, 0:WINDOW],
                                      WQr[:, kc * D + mo * 128: kc * D + (mo + 1) * 128],
                                      XSB[:, kc * HT + col: kc * HT + col + WINDOW],
                                      start=(kc == 0), stop=(kc == NCH - 1))
            p.inc(mm, "mm", 1)
            sv = norm_rope(psQ[:, 0:WINDOW], QP[:, mo * WINDOW:(mo + 1) * WINDOW],
                           None, mo, WINDOW,
                           qtc[:, :], qts[:, :],
                           qsT, "n_v")
            pool.release(bQ, "n_s", sv)
        aqv = p.val("n_s")
        for h in range(16):
            base = (h % 2) * 64
            ch = h // 2
            for qt2 in range(2):
                psS, bS = pool.acquire()
                nc.tensor.wait_ge(p.sem("n_s"), aqv)
                kcol = w_i * WINDOW
                mm = nc.tensor.matmul(psS[:, :],
                                      QP[base:base + 64,
                                         ch * WINDOW + qt2 * 128: ch * WINDOW + qt2 * 128 + 128],
                                      KP[base:base + 64, ch * TK + kcol: ch * TK + kcol + 512],
                                      start=True, stop=False, skip_group_check=True)
                midx = qt2 if w_i > 0 else 2 + qt2
                mm = nc.tensor.matmul(psS[:, :], ident_t[:, :],
                                      masks_t[:, midx * 512:(midx + 1) * 512],
                                      start=False, stop=True, skip_group_check=True)
                mmv = p.inc(mm, "mm", 1)
                nc.vector.wait_ge(p.sem("mm"), mmv)
                v = nc.vector.reduce_max(negmax[:, :], psS[:, :], axis=AX.X)
                rdv = p.inc(v, "a_v", 1)
                nc.scalar.wait_ge(p.sem("a_v"), rdv)
                s = nc.scalar.mul(negmax[:, :], negmax[:, :], -8.0)
                sv = p.inc(s, "a_s", 1)
                p.wait(nc.scalar, "a_v")
                p.wait(nc.scalar, "a_mm")
                s = nc.scalar.activation(ES[:, :], psS[:, :], AF.Exp, scale=8.0,
                                         bias=negmax[:, :])
                sv = p.inc(s, "a_s", 1)
                pool.release(bS, "a_s", sv)
                nc.vector.wait_ge(p.sem("a_s"), sv)
                v = nc.vector.reduce_sum(rsum[:, :], ES[:, :], axis=AX.X)
                rdv = p.inc(v, "a_v", 1)
                nc.scalar.wait_ge(p.sem("a_v"), rdv)
                s = nc.scalar.activation(rsum[:, :], rsum[:, :], AF.Ln)
                p.inc(s, "a_s", 1)
                s = nc.scalar.activation(rsum[:, :], rsum[:, :], AF.Exp, scale=-1.0)
                sv = p.inc(s, "a_s", 1)
                nc.vector.wait_ge(p.sem("a_s"), sv)
                v = nc.vector.tensor_scalar_mul(ES[:, :], ES[:, :], rsum[:, :])
                esv = p.inc(v, "a_v", 1)
                psO, bO = pool.acquire()
                for kc in range(4):
                    nc.tensor.wait_ge(p.sem("a_v"), esv)
                    p.wait(nc.tensor, "a_s")
                    mm = nc.tensor.transpose(psT_t[:, :], ES[:, kc * 128:(kc + 1) * 128],
                                             ident_t[:, :])
                    mmv = p.inc(mm, "a_mm", 1)
                    nc.scalar.wait_ge(p.sem("a_mm"), mmv)
                    s = nc.scalar.copy(EST[:, kc * 128:(kc + 1) * 128], psT_t[:, :])
                    sv2 = p.inc(s, "a_s", 1)
                    nc.tensor.wait_ge(p.sem("a_s"), sv2)
                    tcg = w_i * 2 + kc
                    mm = nc.tensor.matmul(psO[0:64, 0:128],
                                          VT[:, tcg * 1024 + h * 64: tcg * 1024 + (h + 1) * 64],
                                          EST[:, kc * 128:(kc + 1) * 128],
                                          start=(kc == 0), stop=(kc == 3))
                    amv = p.inc(mm, "a_mm", 1)
                nc.scalar.wait_ge(p.sem("a_mm"), amv)
                s = nc.scalar.copy(AO[base:base + 64,
                                      ch * WINDOW + qt2 * 128: ch * WINDOW + qt2 * 128 + 128],
                                   psO[0:64, 0:128])
                sv = p.inc(s, "a_s", 1)
                pool.release(bO, "a_s", sv)
        def epWo(mo, t0, t1, ps, pool_, bank, w_i=w_i):
            par = (w_i * NCH + mo) % 2
            if w_i * NCH + mo >= 2:
                p.wait(nc.sync, "y_v")
            p.inc(nc.sync.dma_start(out=xres[par][:, :],
                                    in_=xl_dram[mo * 128:(mo + 1) * 128,
                                                HB + w_i * WINDOW: HB + (w_i + 1) * WINDOW]),
                  "xld", 16)
            xldv = p.val("xld")
            nc.vector.wait_ge(p.sem("xld"), xldv)
            p.wait(nc.vector, "mm")
            if w_i * NCH + mo >= 2:
                p.wait(nc.vector, "yst")
            v = nc.vector.tensor_add(yo[par][:, :], ps, xres[par][:, :])
            yv = p.inc(v, "y_v", 1)
            pool_.release(bank, "y_v", yv)
            nc.sync.wait_ge(p.sem("y_v"), yv)
            p.inc(nc.sync.dma_start(out=y_dram[mo * 128:(mo + 1) * 128,
                                               w_i * WINDOW:(w_i + 1) * WINDOW],
                                    in_=yo[par][:, :]), "yst", 16)
        AO_rhs = lambda kc, a, b: AO[0:128, kc * WINDOW + a: kc * WINDOW + b]
        gemm(p, f"wo{w_i}", lhsT_dram=Wo_d, K=D, M=D, T=WINDOW, rhs=AO_rhs,
             epilogue=epWo, pool=pool, ws=ws, t_tile=512,
             rhs_ready=("a_s", p.val("a_s")))
    p.wait(nc.sync, "yst")
    res = p.run()
    p.close()
    return [r["y"] for r in res.results]


def run_l0ffn_ffn(xa_parts, d, layer, xouter_parts=None):
    p = Prog()
    nc = p.nc
    xa_dram = p.din("xa", xa_parts)
    xo_dram = p.din("xo", xouter_parts) if xouter_parts is not None else None
    y_dram = p.dout("y", [D, TC])
    W1 = np.asarray(d["L0_ffn_W1"][layer])
    W1a_d = p.din("W1a", f32c(W1[:, 0:2730]))
    W1g_d = p.din("W1g", f32c(W1[:, 2730:5460]))
    W2_d = p.din("W2", f32c(np.asarray(d["L0_ffn_W2"][layer])))
    b1 = np.asarray(d["L0_ffn_b1"][layer])
    b1a_t = load_small(p, "b1a_t", p.din("b1a", chunk_vec(b1[0:2730])), [128, 22])
    b1g_t = load_small(p, "b1g_t", p.din("b1g", chunk_vec(b1[2730:5460])), [128, 22])
    b2_t = load_small(p, "b2_t", p.din("b2", chunk_vec(np.asarray(d["L0_ffn_b2"][layer]))),
                      [128, NCH])
    lng_t = load_small(p, "lng_t", p.din("lng", chunk_vec(np.asarray(d["L0_ffn_ln_g"][layer]))),
                       [128, NCH])
    lnb_t = load_small(p, "lnb_t", p.din("lnb", chunk_vec(np.asarray(d["L0_ffn_ln_b"][layer]))),
                       [128, NCH])
    consts = stage_consts(p)
    pool = PsumPool(p, "pp", banks=6)
    scr = make_ln_scratch(p)
    ws = WStream(p, "ws", 4096)
    u = p.uid()
    TB = 512
    xf = p.sbuf(f"xf{u}", [128, NCH * (TB + 1)], F32)
    xn = p.sbuf(f"xn{u}", [128, NCH * TB], BF16)
    ha = p.sbuf(f"ha{u}", [128, TB], F32)
    hb = p.sbuf(f"hb{u}", [128, TB], F32)
    H = p.sbuf(f"H{u}", [128, 22 * TB], BF16)
    xot = [p.sbuf(f"xot{u}_{i}", [128, TB], F32) for i in range(2)] if xo_dram is not None else None
    yo = [p.sbuf(f"yo{u}_{i}", [128, TB], F32) for i in range(2)]
    p.join_engines(sems=["cld"])
    NM1 = 22
    for blk in range(TC // TB):
        b0 = blk * TB
        p.wait(nc.sync, "ln_v")
        p.wait(nc.sync, "ln_s")
        p.wait(nc.sync, "mm")
        p.wait(nc.sync, "f_v")
        for j in range(NCH):
            p.inc(nc.sync.dma_start(out=xf[:, j * (TB + 1): j * (TB + 1) + TB + 1],
                                    in_=xa_dram[j * 128:(j + 1) * 128, b0: b0 + TB + 1]),
                  "xld", 16)
        p.wait(nc.scalar, "xld")
        p.wait(nc.tensor, "xld")
        p.wait(nc.vector, "xld")
        p.wait(nc.vector, "mm")
        layernorm(p, f"fln{u}_{blk}",
                  X=lambda j, a, b: xf[:, j * (TB + 1) + a + (0 if j >= 4 else 1):
                                       j * (TB + 1) + b + (0 if j >= 4 else 1)],
                  T=TB, scr=scr,
                  out_bf=lambda j, a, b: xn[:, j * TB + a: j * TB + b],
                  gain=lng_t, gbias=lnb_t,
                  pool=pool, ones_f32=consts["ones_f32"], eps_ap=consts["eps"][0:1, 0:1])
        xn_rhs = lambda kc, a, b: xn[0:128, kc * TB + a: kc * TB + b]
        lnv = ("ln_v", p.val("ln_v"))

        def epA(mo, t0, t1, ps, pool_, bank):
            mr = min(128, 2730 - mo * 128)
            p.wait(nc.vector, "mm")
            p.wait(nc.vector, "f_v")
            v = nc.vector.tensor_scalar_add(ha[0:mr, t0:t1], ps, b1a_t[0:mr, mo:mo + 1])
            pool_.release(bank, "f_a", p.inc(v, "f_a", 1))

        def epB(mo, t0, t1, ps, pool_, bank):
            mr = min(128, 2730 - mo * 128)
            p.wait(nc.scalar, "mm")
            p.wait(nc.scalar, "f_v")
            s = nc.scalar.activation(hb[0:mr, t0:t1], ps, AF.Gelu, bias=b1g_t[0:mr, mo:mo + 1])
            sv = p.inc(s, "f_b", 1)
            pool_.release(bank, "f_b", sv)
            nc.vector.wait_ge(p.sem("f_b"), sv)
            nc.vector.wait_ge(p.sem("f_a"), p.val("f_a"))
            v = nc.vector.tensor_tensor(H[0:mr, mo * TB + t0: mo * TB + t1],
                                        ha[0:mr, t0:t1], hb[0:mr, t0:t1], ALU.mult)
            p.inc(v, "f_v", 1)

        for mo in range(NM1):
            mr = min(128, 2730 - mo * 128)
            gemm(p, f"w1a{u}", lhsT_dram=W1a_d[:, mo * 128: mo * 128 + mr],
                 K=D, M=mr, T=TB, rhs=xn_rhs,
                 epilogue=lambda m_, a, b, ps, pl, bk, mo=mo: epA(mo, a, b, ps, pl, bk),
                 pool=pool, ws=ws, t_tile=TB, rhs_ready=lnv)
            gemm(p, f"w1g{u}", lhsT_dram=W1g_d[:, mo * 128: mo * 128 + mr],
                 K=D, M=mr, T=TB, rhs=xn_rhs,
                 epilogue=lambda m_, a, b, ps, pl, bk, mo=mo: epB(mo, a, b, ps, pl, bk),
                 pool=pool, ws=ws, t_tile=TB, rhs_ready=lnv)

        def epO(mo, t0, t1, ps, pool_, bank, blk=blk):
            it = blk * NCH + mo
            par = it % 2
            p.wait(nc.vector, "mm")
            if it >= 2:
                p.wait(nc.vector, "yst")
            if xo_dram is not None:
                if it >= 2:
                    p.wait(nc.sync, "f_v")
                p.inc(nc.sync.dma_start(out=xot[par][:, :],
                                        in_=xo_dram[mo * 128:(mo + 1) * 128, b0:b0 + TB]),
                      "xld2", 16)
                nc.vector.wait_ge(p.sem("xld2"), p.val("xld2"))
            v = nc.vector.scalar_tensor_tensor(yo[par][:, :], ps, b2_t[:, mo:mo + 1],
                                               xf[:, mo * (TB + 1) + 1: mo * (TB + 1) + 1 + TB],
                                               ALU.add, ALU.add)
            yv = p.inc(v, "f_v", 1)
            pool_.release(bank, "f_v", yv)
            if xo_dram is not None:
                v = nc.vector.tensor_add(yo[par][:, :], yo[par][:, :], xot[par][:, :])
                yv = p.inc(v, "f_v", 1)
            nc.sync.wait_ge(p.sem("f_v"), yv)
            p.inc(nc.sync.dma_start(out=y_dram[mo * 128:(mo + 1) * 128, b0:b0 + TB],
                                    in_=yo[par][:, :]), "yst", 16)

        H_rhs = lambda kc, a, b: H[0:min(128, 2730 - kc * 128), kc * TB + a: kc * TB + b]
        gemm(p, f"w2{u}", lhsT_dram=W2_d, K=2730, M=D, T=TB, rhs=H_rhs, epilogue=epO,
             pool=pool, ws=ws, t_tile=TB, rhs_ready=("f_v", p.val("f_v")))
    p.wait(nc.sync, "yst")
    res = p.run()
    p.close()
    return [r["y"] for r in res.results]


def _stitch_halo(parts, nh):
    out = []
    for c in range(NCORE):
        b, q = c // 4, c % 4
        rows = parts[c].shape[0]
        buf = np.zeros((rows, nh + TC), np.float32)
        buf[:, nh:] = parts[c]
        if q > 0:
            buf[:, :nh] = parts[b * 4 + q - 1][:, TC - nh:]
        out.append(f32c(buf))
    return out


def run_l0ffn_app(x_parts, d):
    cur = run_l0ffn_ln(x_parts, d)
    for layer in range(6):
        xl = _stitch_halo(cur, HB)
        xa = run_l0ffn_attn(xl, d, layer)
        xah = _stitch_halo(xa, 1)
        cur = run_l0ffn_ffn(xah, d, layer, xouter_parts=x_parts if layer == 5 else None)
    return cur
